# version 04
# baseline (speedup 1.0000x reference)
import numpy as np
import ml_dtypes
import concourse.bass as bass
import concourse.mybir as mybir
from concourse.bass_utils import run_bass_kernel_spmd

F32 = mybir.dt.float32
BF16 = mybir.dt.bfloat16
AF = mybir.ActivationFunctionType
ALU = mybir.AluOpType
AX = mybir.AxisListType

B, S, D = 2, 8192, 1024
H = 16
DFF, NE = 2816, 8
NB, KB = 16, 512
TOK = 2048
NCORES = 8
NFC = 8
SUB = 256
EPS = 1e-5

ENGINES = ("tensor", "vector", "scalar", "sync", "gpsimd")


class Prog:

    def __init__(self):
        self.steps = {e: [] for e in ENGINES}
        self.count = {}
        self.hwm = {e: {} for e in ENGINES}
        self.own = {"tensor": "pe", "vector": "ve", "scalar": "act"}

    def val(self, sem):
        return self.count.get(sem, 0)

    def add(self, eng, fn, waits=(), inc=None, self_ok=False):
        wl = []
        for s, v in waits:
            assert v <= self.count.get(s, 0), ("forward wait", eng, s, v, self.count.get(s, 0))
            if v <= 0 or (self.own.get(eng) == s and not self_ok):
                continue
            if self.hwm[eng].get(s, -1) >= v:
                continue
            self.hwm[eng][s] = v
            wl.append((s, v))
        if inc is not None:
            s, k = inc
            self.count[s] = self.count.get(s, 0) + k
        self.steps[eng].append((wl, fn, inc))


class Banks:

    def __init__(self):
        self.cons = [dict() for _ in range(8)]
        self.last_grp_bank = -1

    def ww(self, banks):
        w = {}
        for b in banks:
            for s, v in self.cons[b].items():
                w[s] = max(w.get(s, 0), v)
            self.cons[b] = {}
        return list(w.items())

    def consumed(self, banks, sem, val):
        for b in banks:
            self.cons[b][sem] = max(self.cons[b].get(sem, 0), val)


PARANOID = [""]


def build_nc(phase=2, nb=NB):
    nc = bass.Bass("TRN2", target_bir_lowering=False)

    d_xseq = nc.dram_tensor("xseq", [D, S], F32, kind="ExternalInput").ap()
    d_xown = nc.dram_tensor("xown", [D, TOK], F32, kind="ExternalInput").ap()
    d_mask = nc.dram_tensor("mask", [128, 512], BF16, kind="ExternalInput").ap()
    d_ident = nc.dram_tensor("ident", [128, 128], F32, kind="ExternalInput").ap()
    d_wq = nc.dram_tensor("wq", [D, D], F32, kind="ExternalInput").ap()
    d_wk = nc.dram_tensor("wk", [D, D], F32, kind="ExternalInput").ap()
    d_wv = nc.dram_tensor("wv", [D, D], F32, kind="ExternalInput").ap()
    d_wu = nc.dram_tensor("wu", [D, D], F32, kind="ExternalInput").ap()
    d_bu = nc.dram_tensor("bu", [1, D], F32, kind="ExternalInput").ap()
    d_gwT = nc.dram_tensor("gwT", [D, NE], F32, kind="ExternalInput").ap()
    d_egT = nc.dram_tensor("egT", [NE, D, DFF], F32, kind="ExternalInput").ap()
    d_euT = nc.dram_tensor("euT", [NE, D, DFF], F32, kind="ExternalInput").ap()
    d_edT = nc.dram_tensor("edT", [NE, DFF, D], F32, kind="ExternalInput").ap()
    d_sgT = nc.dram_tensor("sgT", [D, DFF], F32, kind="ExternalInput").ap()
    d_suT = nc.dram_tensor("suT", [D, DFF], F32, kind="ExternalInput").ap()
    d_sdT = nc.dram_tensor("sdT", [DFF, D], F32, kind="ExternalInput").ap()
    d_out = nc.dram_tensor("xout", [D, TOK], F32, kind="ExternalOutput").ap()

    d_x1 = nc.dram_tensor("x1sp", [128, NFC, TOK], F32, kind="Internal").ap()
    d_l = nc.dram_tensor("ldram", [H, NB, TOK], F32, kind="Internal").ap()
    d_rl = nc.dram_tensor("rldram", [H, TOK], F32, kind="Internal").ap()

    xseq3 = d_xseq.rearrange("(c p) t -> p c t", p=128)
    xown3 = d_xown.rearrange("(c p) t -> p c t", p=128)
    out3 = d_out.rearrange("(c p) t -> p c t", p=128)
    x13 = d_x1
    wq3 = d_wq.rearrange("(c p) m -> p c m", p=128)
    wk3 = d_wk.rearrange("(c p) m -> p c m", p=128)
    wv3 = d_wv.rearrange("(c p) m -> p c m", p=128)
    wu3 = d_wu.rearrange("(c p) m -> p c m", p=128)

    cur = [16832]
    offs = {}

    def sb(name, shape, dt, at=None, align=64):
        if at is None:
            cur[0] = (cur[0] + align - 1) // align * align
            at = cur[0]
            n = 1
            for s_ in shape[1:]:
                n *= s_
            cur[0] += n * (2 if dt == BF16 else 4)
        h = nc.alloc_sbuf_tensor_at(name, shape, dt, offset=at)
        offs[id(h)] = at
        return h

    def off(h):
        return offs[id(h)]

    const_sb = sb("const_sb", [128, 4], F32)
    nc.const_aps.register(F32, EPS, const_sb[:, 0:1])
    nc.const_aps.register(F32, -0.5, const_sb[:, 1:2])
    ones_bf = sb("ones_bf", [128, 128], BF16)
    ones_f = sb("ones_f", [128, 128], F32)
    sel_f = sb("sel_f", [128, 256], F32)
    bu_col = sb("bu_col", [128, 8], F32)
    mask_sb = sb("mask_sb", [128, 512], BF16)
    rows = sb("rows", [128, 1024], F32)
    rows2 = sb("rows2", [128, 1024], F32)
    ident_sb = sb("ident_sb", [128, 128], F32)
    base_arena = cur[0]

    cur[0] = base_arena
    Q = sb("Q", [128, NFC, TOK], BF16)
    O_sb = sb("O_sb", [128, NFC, TOK], F32)
    wkv = sb("wkv", [128, 2, NFC, 1024], BF16)
    x_blk = sb("x_blk", [128, 2, NFC, SUB], F32)
    xb = sb("xb", [128, 1, NFC, SUB], BF16)
    lnz = sb("lnz", [128, NFC, KB], BF16)
    xsqc = sb("xsqc", [128, 4, SUB], BF16)
    K_fm = sb("K_fm", [128, NFC, KB], BF16)
    V_tm = sb("V_tm", [128, 4, H * 65], BF16)
    PT = sb("PT", [128, 2, 3, 512], BF16)
    mu_b = sb("mu_b", [128, KB], F32)
    rstd_b = sb("rstd_b", [128, KB], F32)
    rstd_c = sb("rstd_c", [128, 4], F32)
    l_stage = sb("l_stage", [128, TOK], F32)
    attn_end = cur[0]

    rstdq_b = sb("rstdq_b", [128, KB], F32, at=off(K_fm))
    l16a = sb("l16a", [16, TOK], F32, at=off(x_blk))
    l16b = sb("l16b", [16, TOK], F32, at=off(lnz))
    y_att = sb("y_att", [128, 2, NFC, SUB], F32, at=off(x_blk))
    O_bf = sb("O_bf", [128, 2, NFC, SUB], BF16, at=off(V_tm))
    ep_t1 = sb("ep_t1", [128, SUB], F32, at=off(PT))
    ep_t2 = sb("ep_t2", [128, SUB], F32, at=off(PT) + 1024)
    ep_bc = sb("ep_bc", [128, 2, SUB], BF16, at=off(PT) + 2048)
    ep_sq = sb("ep_sq", [128, 2, SUB], BF16, at=off(PT) + 3072)
    ep_x = sb("ep_x", [128, NFC, SUB], F32, at=off(l_stage))
    zeros4k = sb("zeros4k", [128, 4096], F32, at=off(O_sb))

    cur[0] = base_arena
    xf = sb("xf", [128, NFC, 1024], BF16)
    h_sb = sb("h_sb", [128, 22, 1024], BF16)
    dw_sb = sb("dw_sb", [128, 22, 1024], BF16)
    y_acc = sb("y_acc", [128, NFC, 1024], F32)
    gwc = sb("gwc", [128, 2, NFC, 128], BF16)
    uwc = sb("uwc", [128, 2, NFC, 128], BF16)
    sg_sb = sb("sg_sb", [128, 1024], BF16)
    tmp_e = sb("tmp_e", [128, 512], F32)
    comb_rows = sb("comb_rows", [128, 3072], F32)
    combT = sb("combT", [8, 1024], F32)
    comb_sb = sb("comb_sb", [128, 8], F32)
    probs_sb = sb("probs_sb", [128, 8], F32)
    eq_sb = sb("eq_sb", [128, 8], F32)
    smalls = sb("smalls", [128, 16], F32)
    gwT32 = sb("gwT32", [128, NFC, 8], F32)
    gsum_b = sb("gsum_b", [128, 8], F32)
    lgt_sb = sb("lgt_sb", [128, 8], F32)
    x1st = sb("x1st", [128, NFC, SUB], F32)
    ybc = sb("ybc", [128, 2, SUB], BF16)
    ysq = sb("ysq", [128, 2, SUB], BF16)
    ctile = sb("ctile", [128, 2, 512], F32)
    ffn_end = cur[0]
    assert max(attn_end, ffn_end) <= 228500, (attn_end, ffn_end)

    pa = nc.alloc_psum_tensor("pa", [128, 4096], F32)

    def bk(b, p0=0, p1=128, w=512):
        return pa[p0:p1, b * 512:b * 512 + w]

    P = Prog()
    BK = Banks()
    rows_pe = {"v": 0}

    def _pw(waits, eng):
        if eng not in PARANOID[0]:
            return waits
        return list(waits) + [(sname, P.val(sname)) for sname in ("pe", "ve", "act", "x", "w")]

    def t_add(fn, waits=(), inc=("pe", 1)):
        P.add("tensor", fn, _pw(waits, "t"), inc)
        return P.val("pe")

    def v_add(fn, waits=(), inc=("ve", 1)):
        P.add("vector", fn, _pw(waits, "v"), inc)
        return P.val("ve")

    def v_seq(fn, waits=(), inc=("ve", 1)):
        P.add("vector", fn, list(waits) + [("ve", P.val("ve"))], inc, self_ok=True)
        return P.val("ve")

    def s_add(fn, waits=(), inc=("act", 1)):
        P.add("scalar", fn, _pw(waits, "s"), inc)
        return P.val("act")

    def dx(fn, waits=()):
        P.add("sync", fn, _pw(waits, "x"), ("x", 16))
        return P.val("x")

    def dw(fn, waits=()):
        P.add("gpsimd", fn, _pw(waits, "w"), ("w", 16))
        return P.val("w")

    mask_rdy = dx(lambda e: e.dma_start(out=mask_sb[:, :], in_=d_mask))
    ident_rdy = dx(lambda e: e.dma_start(out=ident_sb[:, :], in_=d_ident))
    burow_rdy = dx(lambda e: e.dma_start(out=rows2[0:1, 0:1024], in_=d_bu))

    v_add(lambda e: e.memset(const_sb[:, 0:1], EPS))
    v_add(lambda e: e.memset(const_sb[:, 1:2], -0.5))
    v_add(lambda e: e.memset(ones_bf[:, :], 1.0))
    v_add(lambda e: e.memset(ones_f[:, :], 1.0))
    v_add(lambda e: e.memset(sel_f[:, :], 0.0))
    for r in (0, 32, 64):
        v_add(lambda e, r=r: e.memset(sel_f[r:r + 1, 0:64], 1.0))
        v_add(lambda e, r=r: e.memset(sel_f[r:r + 1, 192:256], 1.0))
    init_ve = P.val("ve")

    for of in range(8):
        bsel = of % 2
        t_add(lambda e, of=of, bsel=bsel: e.matmul(
            bk(bsel, w=1), rows2[0:1, of * 128:(of + 1) * 128], ones_f[0:1, 0:1],
            start=True, stop=True),
            waits=BK.ww([bsel]) + [("x", burow_rdy), ("ve", init_ve)])
        pv = P.val("pe")
        v_add(lambda e, of=of, bsel=bsel: e.tensor_copy(bu_col[:, of:of + 1], bk(bsel, w=1)),
              waits=[("pe", pv)])
        BK.consumed([bsel], "ve", P.val("ve"))
    bu_ve = P.val("ve")

    v_add(lambda e: e.memset(zeros4k[:, :], 0.0))
    zv = P.val("ve")
    ld_flat = d_l.rearrange("h j q -> (h j q)").rearrange("(p n) -> p n", p=128)
    ldzero_x = dx(lambda e: e.dma_start(out=ld_flat[:, :], in_=zeros4k[:, :]),
                  waits=[("ve", zv)])

    wq_rdy = dw(lambda e: e.dma_start(out=wkv[:, 0, :, :], in_=wq3))

    v_add(lambda e: e.memset(V_tm[:, :, :], 0.0))
    vt4 = V_tm.rearrange("p a (h c) -> p a h c", c=65)
    v_add(lambda e: e.memset(vt4[:, :, :, 64:65], 1.0))
    vones_ve = P.val("ve")

    xb_consumed = {0: ("pe", 0)}
    xblk_consumed = {0: ("ve", 0), 1: ("ve", 0)}

    def ln_stats(bi, sub_cols, banks, xb_rdy):
        sb_, qb_, mb_, rb_ = banks
        def sum_grp(e):
            last = None
            for fc in range(NFC):
                last = e.matmul(bk(sb_, 0, 1, SUB), ones_bf[:, 0:1], xb[:, 0, fc, :],
                                start=(fc == 0), stop=(fc == NFC - 1))
            return last
        t_add(sum_grp, waits=BK.ww([sb_]) + [("w", xb_rdy)])
        sum_pe = P.val("pe")
        wwq = BK.ww([qb_])
        sqmm_pe = []
        for fc in range(NFC):
            slot = bi * 2 + fc % 2
            wv = [("w", xb_rdy)] + (wwq if fc == 0 else
                                    [("pe", sqmm_pe[fc - 2])] if fc >= 2 else [])
            sv = v_add(lambda e, fc=fc, slot=slot: e.tensor_mul(
                xsqc[:, slot, :], xb[:, 0, fc, :], xb[:, 0, fc, :]), waits=wv)
            t_add(lambda e, fc=fc, slot=slot: e.matmul(
                bk(qb_, 0, 1, SUB), ones_bf[:, 0:1], xsqc[:, slot, :],
                start=(fc == 0), stop=(fc == NFC - 1)), waits=[("ve", sv)])
            sqmm_pe.append(P.val("pe"))
        sq_pe = P.val("pe")
        mu_ve = v_add(lambda e: e.tensor_scalar_mul(rows[0:1, 0:SUB], bk(sb_, 0, 1, SUB), 1.0 / D),
                      waits=[("pe", sum_pe), ("pe", rows_pe["v"])])
        v_add(lambda e: e.tensor_scalar_mul(rows[32:33, 0:SUB], bk(qb_, 0, 1, SUB), 1.0 / D),
              waits=[("pe", sq_pe)])
        v_add(lambda e: e.tensor_mul(rows[32:33, SUB:2 * SUB], rows[0:1, 0:SUB], rows[0:1, 0:SUB]))
        var_ve = v_add(lambda e: e.tensor_sub(rows[64:65, 0:SUB], rows[32:33, 0:SUB],
                                              rows[32:33, SUB:2 * SUB]))
        s_add(lambda e: e.activation(rows[64:65, SUB:2 * SUB], rows[64:65, 0:SUB], AF.Ln, bias=EPS),
              waits=[("ve", var_ve)])
        rstd_act = s_add(lambda e: e.activation(rows[64:65, 0:SUB], rows[64:65, SUB:2 * SUB],
                                                AF.Exp, scale=-0.5))
        t_add(lambda e: e.matmul(bk(mb_, w=SUB), ones_f[0:1, 0:128], rows[0:1, 0:SUB],
                                 start=True, stop=True), waits=BK.ww([mb_]) + [("ve", mu_ve)])
        mub_pe = P.val("pe")
        t_add(lambda e: e.matmul(bk(rb_, w=SUB), ones_f[64:65, 0:128], rows[64:65, 0:SUB],
                                 start=True, stop=True), waits=BK.ww([rb_]) + [("act", rstd_act)])
        rb_pe = P.val("pe")
        rows_pe["v"] = rb_pe
        v_add(lambda e: e.tensor_copy(mu_b[:, sub_cols], bk(mb_, w=SUB)), waits=[("pe", mub_pe)])
        BK.consumed([mb_], "ve", P.val("ve"))
        v_add(lambda e: e.tensor_copy(rstd_b[:, sub_cols], bk(rb_, w=SUB)), waits=[("pe", rb_pe)])
        BK.consumed([rb_, sb_, qb_], "ve", P.val("ve"))
        return rstd_act

    def ln_z(bi, sub_cols, xblk_rdy):
        for fc in range(NFC):
            v_add(lambda e, fc=fc: e.tensor_sub(lnz[:, fc, sub_cols], x_blk[:, bi, fc, :],
                                                mu_b[:, sub_cols]),
                  waits=[("x", xblk_rdy)])
        return P.val("ve")

    def proj_grp(e, wi, ofc, width, bank):
        last = None
        for fc in range(NFC):
            last = e.matmul(bk(bank, w=width), wkv[:, wi, fc, ofc * 128:(ofc + 1) * 128],
                            lnz[:, fc, 0:width], start=(fc == 0), stop=(fc == NFC - 1))
        return last

    qproj_done_pe = 0
    for blk in range(4):
        z_end = 0
        for half in range(2):
            subi = blk * 2 + half
            bi = subi % 2
            cols = slice(half * SUB, (half + 1) * SUB)
            xr = dx(lambda e, subi=subi, bi=bi: e.dma_start(
                out=x_blk[:, bi, :, :], in_=xown3[:, :, subi * SUB:(subi + 1) * SUB]),
                waits=[xblk_consumed[bi]])
            br = dw(lambda e, subi=subi, bi=bi: e.dma_start(
                out=xb[:, 0, :, :], in_=xown3[:, :, subi * SUB:(subi + 1) * SUB]),
                waits=[xb_consumed[0]])
            banks = (0, 1, 2, 3) if bi == 0 else (4, 5, 6, 7)
            ln_stats(bi, cols, banks, br)
            xb_consumed[0] = ("pe", P.val("pe"))
            rq_ve = v_add(lambda e: e.tensor_scalar_mul(rows[32:33, 0:SUB],
                                                        rows[64:65, 0:SUB], 1.0 / 64))
            t_add(lambda e, banks=banks: e.matmul(
                bk(banks[2], w=SUB), ones_f[32:33, 0:128], rows[32:33, 0:SUB],
                start=True, stop=True), waits=BK.ww([banks[2]]) + [("ve", rq_ve)])
            rq_pe = P.val("pe")
            rows_pe["v"] = rq_pe
            v_add(lambda e, cols=cols, banks=banks: e.tensor_copy(
                rstdq_b[:, cols], bk(banks[2], w=SUB)), waits=[("pe", rq_pe)])
            BK.consumed([banks[2]], "ve", P.val("ve"))
            zv = ln_z(bi, cols, xr)
            xblk_consumed[bi] = ("ve", zv)
            z_end = max(z_end, zv)
        for qf in range(NFC):
            bsel = (qf % 4) * 2
            t_add(lambda e, qf=qf, bsel=bsel: proj_grp(e, 0, qf, KB, bsel),
                  waits=BK.ww([bsel]) + [("ve", z_end), ("w", wq_rdy)])
            ppe = P.val("pe")
            v_add(lambda e, qf=qf, blk=blk, bsel=bsel: e.tensor_tensor(
                Q[:, qf, blk * KB:(blk + 1) * KB], bk(bsel), rstdq_b[:, :], ALU.mult),
                  waits=[("pe", ppe)])
            BK.consumed([bsel], "ve", P.val("ve"))
        qproj_done_pe = P.val("pe")
    q_ready_ve = P.val("ve")

    wk_rdy = dw(lambda e: e.dma_start(out=wkv[:, 0, :, :], in_=wk3), waits=[("pe", qproj_done_pe)])
    wv_rdy = dw(lambda e: e.dma_start(out=wkv[:, 1, :, :], in_=wv3))

    TGRP = [[(0, 0), (0, 1)], [(1, 0), (1, 1)], [(2, 0), (2, 1)], [(3, 0), (3, 1)]]
    state = {
        "kfm_pe": 0, "vtm_pe": 0, "o_ve": 0,
        "lslot_free": [("x", 0), ("x", 0), ("x", 0)], "lslot": 0,
        "pt_free": [("pe", 0), ("pe", 0)],
    }

    def kv_phase(j):
        z_end = 0
        for half in range(2):
            bi = half
            cols = slice(half * SUB, (half + 1) * SUB)
            tok0 = j * KB + half * SUB
            xr = dx(lambda e, bi=bi, tok0=tok0: e.dma_start(
                out=x_blk[:, bi, :, :], in_=xseq3[:, :, tok0:tok0 + SUB]),
                waits=[xblk_consumed[bi]])
            br = dw(lambda e, bi=bi, tok0=tok0: e.dma_start(
                out=xb[:, 0, :, :], in_=xseq3[:, :, tok0:tok0 + SUB]),
                waits=[xb_consumed[0]])
            banks = (0, 1, 2, 3) if bi == 0 else (4, 5, 6, 7)
            rstd_act = ln_stats(bi, cols, banks, br)
            xb_consumed[0] = ("pe", P.val("pe"))
            for k2 in range(2):
                kcc = half * 2 + k2
                t_add(lambda e, k2=k2, banks=banks: e.matmul(
                    bk(banks[2], w=1), rows[64:65, k2 * 128:(k2 + 1) * 128], ones_f[64:65, 0:1],
                    start=True, stop=True),
                    waits=BK.ww([banks[2]]) + [("act", rstd_act)])
                cpe = P.val("pe")
                rows_pe["v"] = cpe
                v_add(lambda e, kcc=kcc, banks=banks: e.tensor_copy(
                    rstd_c[:, kcc:kcc + 1], bk(banks[2], w=1)), waits=[("pe", cpe)])
                BK.consumed([banks[2]], "ve", P.val("ve"))
            zv = ln_z(bi, cols, xr)
            xblk_consumed[bi] = ("ve", zv)
            z_end = max(z_end, zv)
        for kf in range(NFC):
            bsel = (kf % 4) * 2
            t_add(lambda e, kf=kf, bsel=bsel: proj_grp(e, 0, kf, KB, bsel),
                  waits=BK.ww([bsel]) + [("ve", z_end), ("w", wk_rdy), ("pe", state["kfm_pe"])])
            ppe = P.val("pe")
            v_add(lambda e, kf=kf, bsel=bsel: e.tensor_tensor(
                K_fm[:, kf, :], bk(bsel), rstd_b[:, :], ALU.mult), waits=[("pe", ppe)])
            BK.consumed([bsel], "ve", P.val("ve"))
        for tk in range(4):
            for vg in range(2):
                bsel = ((tk * 2 + vg) % 4) * 2 + 1
                def vmm(e, tk=tk, vg=vg, bsel=bsel):
                    last = None
                    for fc in range(NFC):
                        last = e.matmul(bk(bsel), lnz[:, fc, tk * 128:(tk + 1) * 128],
                                        wkv[:, 1, fc, vg * 512:(vg + 1) * 512],
                                        start=(fc == 0), stop=(fc == NFC - 1))
                    return last
                t_add(vmm, waits=BK.ww([bsel]) + [("ve", max(z_end, vones_ve)),
                                                  ("w", wv_rdy), ("pe", state["vtm_pe"])])
                ppe = P.val("pe")
                v_add(lambda e, tk=tk, vg=vg, bsel=bsel: e.tensor_scalar(
                    vt4[:, tk, vg * 8:(vg + 1) * 8, 0:64],
                    bk(bsel).rearrange("p (h c) -> p h c", c=64),
                    rstd_c[:, tk:tk + 1], None, ALU.mult), waits=[("pe", ppe)])
                BK.consumed([bsel], "ve", P.val("ve"))
        return P.val("ve")

    def heads_phase(j, kv_ve):
        prefix = TOK - 128 * j
        widths = []
        off = 0
        while off < prefix:
            w = min(512, prefix - off)
            widths.append((off, w))
            off += w
        qn_ctr = [0]
        for hp in range(NFC):
            hA, hB = 2 * hp, 2 * hp + 1
            av_pe = 0
            for (qoff, W) in widths:
                is_last = (qoff + W == prefix)
                avb6 = 4 if qn_ctr[0] % 2 == 0 else 6
                qn_ctr[0] += 1
                wwav = BK.ww([avb6, avb6 + 1])
                first_av = [True, True]
                for gi, grp in enumerate(TGRP):
                    sbanks = (0, 1) if gi % 2 == 0 else (2, 3)
                    pt_i = gi % 2
                    nsl = len(grp)
                    def s_mms(e, grp=grp, sbanks=sbanks, hp=hp, qoff=qoff, W=W):
                        last = None
                        for si, (kcc, hb) in enumerate(grp):
                            p0 = hb * 64
                            last = e.matmul(
                                bk(sbanks[si], w=W),
                                K_fm[p0:p0 + 64, hp, kcc * 128:(kcc + 1) * 128],
                                Q[p0:p0 + 64, hp, qoff:qoff + W],
                                start=True, stop=True, tile_position=(p0, 0))
                        return last
                    t_add(s_mms, waits=BK.ww(list(sbanks)[:nsl]) +
                          [("ve", max(kv_ve, q_ready_ve)), state["pt_free"][pt_i]])
                    s_pe = P.val("pe")
                    def expf(e, sbanks=sbanks, nsl=nsl, pt_i=pt_i, W=W):
                        src = pa[:, sbanks[0] * 512:sbanks[0] * 512 + nsl * 512]
                        src = src.rearrange("p (n w) -> p n w", w=512)[:, :, 0:W]
                        return e.activation(PT[:, pt_i, 0:nsl, 0:W], src, AF.Exp)
                    e_act = s_add(expf, waits=[("pe", s_pe)])
                    BK.consumed(list(sbanks)[:nsl], "act", e_act)
                    m_ve = None
                    if is_last:
                        for si, (kcc, hb) in enumerate(grp):
                            v_add(lambda e, si=si, kcc=kcc, pt_i=pt_i, W=W: e.tensor_mul(
                                PT[:, pt_i, si, W - 128:W], PT[:, pt_i, si, W - 128:W],
                                mask_sb[:, kcc * 128:(kcc + 1) * 128]),
                                  waits=[("act", e_act), ("x", mask_rdy)])
                        m_ve = P.val("ve")
                    def av_mms(e, grp=grp, pt_i=pt_i, hp=hp, W=W, fa=tuple(first_av),
                               avb6=avb6):
                        last = None
                        f = list(fa)
                        for si, (kcc, hb) in enumerate(grp):
                            hh = 2 * hp + hb
                            last = e.matmul(
                                bk(avb6 + hb, 0, 65, W),
                                V_tm[:, kcc, hh * 65:hh * 65 + 65],
                                PT[:, pt_i, si, 0:W],
                                start=f[hb], stop=(kcc == 3))
                            f[hb] = False
                        return last
                    avw = ([("ve", m_ve)] if m_ve else [("act", e_act)]) + \
                          (wwav if gi == 0 else [])
                    t_add(av_mms, waits=avw)
                    for (kcc, hb) in grp:
                        first_av[hb] = False
                    av_pe = P.val("pe")
                    state["pt_free"][pt_i] = ("pe", av_pe)
                oadd = "copy" if j == 0 else "add"
                ow = [("pe", av_pe)] + ([("x", ldzero_x)] if j == 0 else [])
                if oadd == "copy":
                    v_add(lambda e, hp=hp, qoff=qoff, W=W, avb6=avb6: e.tensor_copy(
                        O_sb[0:64, hp, qoff:qoff + W], bk(avb6, 0, 64, W)), waits=ow)
                    v_add(lambda e, hp=hp, qoff=qoff, W=W, avb6=avb6: e.tensor_copy(
                        O_sb[64:128, hp, qoff:qoff + W], bk(avb6 + 1, 0, 64, W)))
                else:
                    v_add(lambda e, hp=hp, qoff=qoff, W=W, avb6=avb6: e.tensor_add(
                        O_sb[0:64, hp, qoff:qoff + W], O_sb[0:64, hp, qoff:qoff + W],
                        bk(avb6, 0, 64, W)), waits=ow)
                    v_add(lambda e, hp=hp, qoff=qoff, W=W, avb6=avb6: e.tensor_add(
                        O_sb[64:128, hp, qoff:qoff + W], O_sb[64:128, hp, qoff:qoff + W],
                        bk(avb6 + 1, 0, 64, W)))
                sA = state["lslot"] % 3
                sB = (state["lslot"] + 1) % 3
                v_add(lambda e, sA=sA, qoff=qoff, W=W, avb6=avb6: e.tensor_copy(
                    l_stage[sA * 32:sA * 32 + 1, qoff:qoff + W], bk(avb6, 64, 65, W)),
                      waits=[state["lslot_free"][sA]])
                v_add(lambda e, sB=sB, qoff=qoff, W=W, avb6=avb6: e.tensor_copy(
                    l_stage[sB * 32:sB * 32 + 1, qoff:qoff + W], bk(avb6 + 1, 64, 65, W)),
                      waits=[state["lslot_free"][sB]])
                state["o_ve"] = P.val("ve")
                BK.consumed([avb6, avb6 + 1], "ve", state["o_ve"])
            sA = state["lslot"] % 3
            sB = (state["lslot"] + 1) % 3
            state["lslot"] += 2
            dx(lambda e, hA=hA, j=j, sA=sA, prefix=prefix: e.dma_start(
                out=d_l[hA, j:j + 1, 0:prefix], in_=l_stage[sA * 32:sA * 32 + 1, 0:prefix]),
               waits=[("ve", state["o_ve"]), ("x", ldzero_x)])
            state["lslot_free"][sA] = ("x", P.val("x"))
            dx(lambda e, hB=hB, j=j, sB=sB, prefix=prefix: e.dma_start(
                out=d_l[hB, j:j + 1, 0:prefix], in_=l_stage[sB * 32:sB * 32 + 1, 0:prefix]))
            state["lslot_free"][sB] = ("x", P.val("x"))
        state["kfm_pe"] = P.val("pe")
        state["vtm_pe"] = P.val("pe")

    for j in range(nb):
        kvv = kv_phase(j)
        heads_phase(j, kvv)

    wu_rdy = dw(lambda e: e.dma_start(out=wkv[:, 0, :, :], in_=wu3),
                waits=[("pe", state["kfm_pe"])])

    l16 = [l16a, l16b]
    l16_free = [("pe", 0), ("pe", 0)]
    prev_spill = 0
    rl_head_x = {}
    for hh in range(H):
        bi = hh % 2
        lr = dx(lambda e, hh=hh, bi=bi: e.dma_start(out=l16[bi][:, :], in_=d_l[hh, :, :]),
                waits=[l16_free[bi], ("ve", state["o_ve"]), ("x", ldzero_x)])
        for ch in range(4):
            bsel = ch % 2
            t_add(lambda e, bi=bi, ch=ch, bsel=bsel: e.matmul(
                bk(bsel, 0, 1, 512), ones_f[0:16, 0:1], l16[bi][0:16, ch * 512:(ch + 1) * 512],
                start=True, stop=True), waits=BK.ww([bsel]) + [("x", lr)])
            mpe = P.val("pe")
            rv = v_add(lambda e, bsel=bsel: e.reciprocal(rows2[0:1, 0:512], bk(bsel, 0, 1, 512)),
                       waits=[("pe", mpe), ("x", prev_spill)])
            BK.consumed([bsel], "ve", rv)
            prev_spill = dx(lambda e, hh=hh, ch=ch: e.dma_start(
                out=d_rl[hh:hh + 1, ch * 512:(ch + 1) * 512], in_=rows2[0:1, 0:512]),
                waits=[("ve", rv)])
        rl_head_x[hh] = prev_spill
        l16_free[bi] = ("pe", P.val("pe"))
    rl_x = P.val("x")

    epx_free = ("x", 0)
    yatt_free = {0: 0, 1: 0}
    obf_free_pe = 0
    rt_prev = {0: 0, 32: 0, 64: 0}
    for tg in range(8):
        bi = tg % 2
        cols = slice(tg * SUB, (tg + 1) * SUB)
        xr = dx(lambda e, tg=tg: e.dma_start(out=ep_x[:, :, :],
                                             in_=xown3[:, :, tg * SUB:(tg + 1) * SUB]),
                waits=[epx_free, ("ve", state["o_ve"])])
        for hp in range(NFC):
            rbase = (0, 32, 64)[hp % 3]
            r1 = dx(lambda e, hp=hp, tg=tg, rbase=rbase: e.dma_start(
                out=rows2[rbase:rbase + 1, 512:512 + SUB],
                in_=d_rl[2 * hp:2 * hp + 1, tg * SUB:(tg + 1) * SUB]),
                waits=[("x", rl_head_x[2 * hp + 1]), ("pe", rt_prev[rbase])])
            r2 = dx(lambda e, hp=hp, tg=tg, rbase=rbase: e.dma_start(
                out=rows2[rbase:rbase + 1, 512 + SUB:512 + 2 * SUB],
                in_=d_rl[2 * hp + 1:2 * hp + 2, tg * SUB:(tg + 1) * SUB]))
            bsel = 2 + hp % 2
            def rtile(e, rbase=rbase, bsel=bsel):
                e.matmul(bk(bsel, w=SUB), sel_f[rbase:rbase + 1, 0:128],
                         rows2[rbase:rbase + 1, 512:512 + SUB], start=True, stop=False)
                return e.matmul(bk(bsel, w=SUB), sel_f[rbase:rbase + 1, 128:256],
                                rows2[rbase:rbase + 1, 512 + SUB:512 + 2 * SUB],
                                start=False, stop=True)
            t_add(rtile, waits=BK.ww([bsel]) + [("x", r2)])
            rpe = P.val("pe")
            rt_prev[rbase] = rpe
            v_add(lambda e, hp=hp, bi=bi, cols=cols, bsel=bsel: e.tensor_tensor(
                O_bf[:, bi, hp, :], O_sb[:, hp, cols], bk(bsel, w=SUB), ALU.mult),
                  waits=[("pe", max(rpe, obf_free_pe))])
            BK.consumed([bsel], "ve", P.val("ve"))
        obf_ve = P.val("ve")
        for of in range(NFC):
            bsel = 4 + of % 2
            def oproj(e, of=of, bi=bi, bsel=bsel):
                last = None
                for af in range(NFC):
                    last = e.matmul(bk(bsel, w=SUB), wkv[:, 0, af, of * 128:(of + 1) * 128],
                                    O_bf[:, bi, af, :], start=(af == 0), stop=(af == NFC - 1))
                return last
            t_add(oproj, waits=BK.ww([bsel]) + [("ve", obf_ve), ("w", wu_rdy)])
            ppe = P.val("pe")
            s_add(lambda e, of=of, bi=bi, bsel=bsel: e.activation(
                y_att[:, bi, of, :], bk(bsel, w=SUB), AF.Identity, bias=bu_col[:, of:of + 1]),
                  waits=[("pe", ppe), ("ve", bu_ve), ("x", yatt_free[bi])])
            BK.consumed([bsel], "act", P.val("act"))
        ya_act = P.val("act")
        obf_free_pe = P.val("pe")
        sumb, sqb, mb_, rb_ = (0, 1, 2, 3) if bi == 0 else (4, 5, 6, 7)
        cw = []
        wws, wwq = BK.ww([sumb]), BK.ww([sqb])
        for fc in range(NFC):
            sl = fc % 2
            v_add(lambda e, fc=fc, bi=bi, sl=sl: e.tensor_copy(ep_bc[:, sl, :], y_att[:, bi, fc, :]),
                  waits=[("act", ya_act)] + (wws if fc == 0 else [("pe", cw[fc - 2])] if fc >= 2 else []))
            bv = P.val("ve")
            t_add(lambda e, fc=fc, sl=sl, sumb=sumb: e.matmul(
                bk(sumb, 0, 1, SUB), ones_bf[:, 0:1], ep_bc[:, sl, :],
                start=(fc == 0), stop=(fc == NFC - 1)), waits=[("ve", bv)])
            cw.append(P.val("pe"))
        sum_pe = P.val("pe")
        cw2 = []
        for fc in range(NFC):
            sl = fc % 2
            v_add(lambda e, fc=fc, bi=bi, sl=sl: e.tensor_mul(
                ep_sq[:, sl, :], y_att[:, bi, fc, :], y_att[:, bi, fc, :]),
                  waits=(wwq if fc == 0 else [("pe", cw2[fc - 2])] if fc >= 2 else []))
            sv = P.val("ve")
            t_add(lambda e, fc=fc, sl=sl, sqb=sqb: e.matmul(
                bk(sqb, 0, 1, SUB), ones_bf[:, 0:1], ep_sq[:, sl, :],
                start=(fc == 0), stop=(fc == NFC - 1)), waits=[("ve", sv)])
            cw2.append(P.val("pe"))
        sq_pe = P.val("pe")
        mu_ve = v_add(lambda e, sumb=sumb: e.tensor_scalar_mul(
            rows[0:1, 0:SUB], bk(sumb, 0, 1, SUB), 1.0 / D),
            waits=[("pe", sum_pe), ("pe", rows_pe["v"])])
        v_add(lambda e, sqb=sqb: e.tensor_scalar_mul(rows[32:33, 0:SUB],
                                                     bk(sqb, 0, 1, SUB), 1.0 / D),
              waits=[("pe", sq_pe)])
        v_add(lambda e: e.tensor_mul(rows[32:33, SUB:2 * SUB], rows[0:1, 0:SUB], rows[0:1, 0:SUB]))
        var_ve = v_add(lambda e: e.tensor_sub(rows[64:65, 0:SUB], rows[32:33, 0:SUB],
                                              rows[32:33, SUB:2 * SUB]))
        s_add(lambda e: e.activation(rows[64:65, SUB:2 * SUB], rows[64:65, 0:SUB], AF.Ln, bias=EPS),
              waits=[("ve", var_ve)])
        r_act = s_add(lambda e: e.activation(rows[64:65, 0:SUB], rows[64:65, SUB:2 * SUB],
                                             AF.Exp, scale=-0.5))
        t_add(lambda e, mb_=mb_: e.matmul(bk(mb_, w=SUB), ones_f[0:1, 0:128], rows[0:1, 0:SUB],
                                          start=True, stop=True),
              waits=BK.ww([mb_]) + [("ve", mu_ve)])
        mpe = P.val("pe")
        t_add(lambda e, rb_=rb_: e.matmul(bk(rb_, w=SUB), ones_f[64:65, 0:128],
                                          rows[64:65, 0:SUB], start=True, stop=True),
              waits=BK.ww([rb_]) + [("act", r_act)])
        rpe = P.val("pe")
        rows_pe["v"] = rpe
        for fc in range(NFC):
            v_add(lambda e, fc=fc, bi=bi, mb_=mb_: e.tensor_sub(
                ep_t1[:, :], y_att[:, bi, fc, :], bk(mb_, w=SUB)),
                  waits=[("pe", max(mpe, rpe)), ("x", xr)])
            v_add(lambda e, rb_=rb_: e.tensor_tensor(ep_t2[:, :], ep_t1[:, :],
                                                     bk(rb_, w=SUB), ALU.mult))
            v_add(lambda e, fc=fc, bi=bi: e.tensor_add(y_att[:, bi, fc, :], ep_t2[:, :],
                                                       ep_x[:, fc, :]))
        x1v = P.val("ve")
        BK.consumed([sumb, sqb, mb_, rb_], "ve", x1v)
        epx_free = ("ve", x1v)
        dx(lambda e, tg=tg, bi=bi: e.dma_start(
            out=x13[:, :, tg * SUB:(tg + 1) * SUB], in_=y_att[:, bi, :, :]),
           waits=[("ve", x1v)])
        if phase == 1:
            dx(lambda e, tg=tg, bi=bi: e.dma_start(
                out=out3[:, :, tg * SUB:(tg + 1) * SUB], in_=y_att[:, bi, :, :]))
        yatt_free[bi] = P.val("x")

    if phase == 1:
        fin = P.val("x")
        P.add("sync", lambda e: e.nop(), waits=[("x", fin)])
        return _emit(nc, P)

    barrier = [("pe", P.val("pe")), ("ve", P.val("ve")), ("act", P.val("act")),
               ("x", P.val("x")), ("w", P.val("w"))]
    gw_rdy = dx(lambda e: e.dma_start(out=gwT32[:, :, :],
                                      in_=d_gwT.rearrange("(c p) m -> p c m", p=128)),
                waits=barrier)
    def gsum_mms(e):
        last = None
        for fc in range(NFC):
            last = e.matmul(bk(0, 0, 1, 8), ones_f[:, 0:1], gwT32[:, fc, :],
                            start=(fc == 0), stop=(fc == NFC - 1))
        return last
    t_add(gsum_mms, waits=BK.ww([0]) + [("x", gw_rdy)])
    gs_pe = P.val("pe")
    gs_ve = v_add(lambda e: e.tensor_copy(rows2[0:1, 0:8], bk(0, 0, 1, 8)),
                  waits=[("pe", gs_pe)])
    BK.consumed([0], "ve", gs_ve)
    t_add(lambda e: e.matmul(bk(1, w=8), ones_f[0:1, 0:128], rows2[0:1, 0:8],
                             start=True, stop=True), waits=BK.ww([1]) + [("ve", gs_ve)])
    gsb_pe = P.val("pe")
    gsum_ve = v_add(lambda e: e.tensor_copy(gsum_b[:, :], bk(1, w=8)), waits=[("pe", gsb_pe)])
    BK.consumed([1], "ve", gsum_ve)

    out_parts = []
    h_ve = 0
    hcons_pe = barrier[0][1]
    guv_prev = {}
    prev_comb_x = 0
    for half in range(2):
        hoff = half * 1024
        xf_ve = 0
        comb_ve = 0
        prev_pact = 0
        x1st_free = [("ve", 0)]
        for sub4 in range(4):
            bi = sub4 % 2
            scol = slice(sub4 * SUB, (sub4 + 1) * SUB)
            xr = dx(lambda e, sub4=sub4, hoff=hoff: e.dma_start(
                out=x1st[:, :, :], in_=x13[:, :, hoff + sub4 * SUB:hoff + (sub4 + 1) * SUB]),
                waits=barrier + x1st_free)
            sumb, sqb, mb_, rb_ = (0, 1, 2, 3) if bi == 0 else (4, 5, 6, 7)
            wws, wwq = BK.ww([sumb]), BK.ww([sqb])
            cw = []
            for fc in range(NFC):
                sl = fc % 2
                v_add(lambda e, fc=fc, sl=sl: e.tensor_copy(ybc[:, sl, :], x1st[:, fc, :]),
                      waits=[("x", xr)] + (wws if fc == 0 else [("pe", cw[fc - 2])] if fc >= 2 else []))
                bv = P.val("ve")
                t_add(lambda e, fc=fc, sl=sl, sumb=sumb: e.matmul(
                    bk(sumb, 0, 1, SUB), ones_bf[:, 0:1], ybc[:, sl, :],
                    start=(fc == 0), stop=(fc == NFC - 1)), waits=[("ve", bv)])
                cw.append(P.val("pe"))
            sum_pe = P.val("pe")
            cw2 = []
            for fc in range(NFC):
                sl = fc % 2
                v_add(lambda e, fc=fc, sl=sl: e.tensor_mul(
                    ysq[:, sl, :], x1st[:, fc, :], x1st[:, fc, :]),
                      waits=(wwq if fc == 0 else [("pe", cw2[fc - 2])] if fc >= 2 else []))
                sv = P.val("ve")
                t_add(lambda e, fc=fc, sl=sl, sqb=sqb: e.matmul(
                    bk(sqb, 0, 1, SUB), ones_bf[:, 0:1], ysq[:, sl, :],
                    start=(fc == 0), stop=(fc == NFC - 1)), waits=[("ve", sv)])
                cw2.append(P.val("pe"))
            sq_pe = P.val("pe")
            mu_ve = v_add(lambda e, sumb=sumb: e.tensor_scalar_mul(
                rows[0:1, 0:SUB], bk(sumb, 0, 1, SUB), 1.0 / D),
                waits=[("pe", sum_pe), ("pe", rows_pe["v"])])
            v_add(lambda e, sqb=sqb: e.tensor_scalar_mul(rows[32:33, 0:SUB],
                                                         bk(sqb, 0, 1, SUB), 1.0 / D),
                  waits=[("pe", sq_pe)])
            v_add(lambda e: e.tensor_mul(rows[32:33, SUB:2 * SUB],
                                         rows[0:1, 0:SUB], rows[0:1, 0:SUB]))
            var_ve = v_add(lambda e: e.tensor_sub(rows[64:65, 0:SUB], rows[32:33, 0:SUB],
                                                  rows[32:33, SUB:2 * SUB]))
            s_add(lambda e: e.activation(rows[64:65, SUB:2 * SUB], rows[64:65, 0:SUB],
                                         AF.Ln, bias=EPS), waits=[("ve", var_ve)])
            r_act = s_add(lambda e: e.activation(rows[64:65, 0:SUB], rows[64:65, SUB:2 * SUB],
                                                 AF.Exp, scale=-0.5))
            t_add(lambda e, mb_=mb_: e.matmul(bk(mb_, w=SUB), ones_f[0:1, 0:128],
                                              rows[0:1, 0:SUB], start=True, stop=True),
                  waits=BK.ww([mb_]) + [("ve", mu_ve)])
            mpe = P.val("pe")
            t_add(lambda e, rb_=rb_: e.matmul(bk(rb_, w=SUB), ones_f[64:65, 0:128],
                                              rows[64:65, 0:SUB], start=True, stop=True),
                  waits=BK.ww([rb_]) + [("act", r_act)])
            rpe = P.val("pe")
            rows_pe["v"] = rpe
            for fc in range(NFC):
                v_add(lambda e, fc=fc, mb_=mb_: e.tensor_sub(
                    tmp_e[:, 0:SUB], x1st[:, fc, :], bk(mb_, w=SUB)),
                      waits=[("pe", max(mpe, rpe))])
                v_add(lambda e, fc=fc, scol=scol, rb_=rb_: e.tensor_tensor(
                    xf[:, fc, scol], tmp_e[:, 0:SUB], bk(rb_, w=SUB), ALU.mult))
            xf_ve = P.val("ve")
            BK.consumed([sumb, sqb, mb_, rb_], "ve", xf_ve)
            if phase == 8 and half == 1 and sub4 == 3:
                dbg_pe, dbg_ve, dbg_act = P.val("pe"), P.val("ve"), P.val("act")
                dx(lambda e: e.dma_start(out=out3[:, :, 0:SUB], in_=x1st[:, :, :]),
                   waits=[("pe", dbg_pe), ("ve", dbg_ve), ("act", dbg_act)])
                dx(lambda e: e.dma_start(out=out3[:, 7, 1024:2048], in_=rows[:, :]))
                fin8 = P.val("x")
                P.add("sync", lambda e: e.nop(), waits=[("x", fin8)])
                return _emit(nc, P)
            for tc2 in ([] if phase == 6 else range(2)):
                tc = sub4 * 2 + tc2
                mw_x = BK.ww([mb_]) + [("ve", mu_ve), ("pe", P.val("pe")),
                                       ("ve", P.val("ve")), ("act", P.val("act"))]
                t_add(lambda e, tc2=tc2: e.matmul(bk(mb_, 0, 128, 1),
                                                  rows[0:1, tc2 * 128:(tc2 + 1) * 128],
                                                  ones_f[0:1, 0:1], start=True, stop=True),
                      waits=mw_x)
                mc_pe = P.val("pe")
                v_add(lambda e: e.tensor_copy(smalls[:, 8:9], bk(mb_, 0, 128, 1)),
                      waits=[("pe", mc_pe)])
                BK.consumed([mb_], "ve", P.val("ve"))
                rw_x = BK.ww([rb_]) + [("act", r_act), ("pe", P.val("pe")),
                                       ("ve", P.val("ve")), ("act", P.val("act"))]
                t_add(lambda e, tc2=tc2: e.matmul(bk(rb_, 0, 128, 1),
                                                  rows[64:65, tc2 * 128:(tc2 + 1) * 128],
                                                  ones_f[64:65, 0:1], start=True, stop=True),
                      waits=rw_x)
                rc_pe = P.val("pe")
                rows_pe["v"] = rc_pe
                v_add(lambda e: e.tensor_copy(smalls[:, 9:10], bk(rb_, 0, 128, 1)),
                      waits=[("pe", rc_pe)])
                BK.consumed([rb_], "ve", P.val("ve"))
                v_seq(lambda e: e.tensor_scalar_mul(smalls[:, 10:11], smalls[:, 9:10], -1.0))
                bsel = 4 + tc % 2 if bi == 0 else tc % 2
                def logit_mms(e, tc2=tc2, bsel=bsel):
                    last = None
                    for fc in range(NFC):
                        last = e.matmul(bk(bsel, w=8), x1st[:, fc, tc2 * 128:(tc2 + 1) * 128],
                                        gwT32[:, fc, :], start=(fc == 0), stop=(fc == NFC - 1))
                    return last
                lw_x = BK.ww([bsel]) + [("x", xr), ("ve", gsum_ve)]
                if "3" in PARANOID[0]:
                    lw_x += [("pe", P.val("pe")), ("ve", P.val("ve")), ("act", P.val("act"))]
                t_add(logit_mms, waits=lw_x)
                lg_pe = P.val("pe")
                q_ve = v_seq(lambda e, bsel=bsel: e.scalar_tensor_tensor(
                    lgt_sb[:, :], gsum_b[:, :], smalls[:, 8:9], bk(bsel, w=8),
                    ALU.mult, ALU.subtract), waits=[("pe", lg_pe), ("act", prev_pact)])
                BK.consumed([bsel], "ve", q_ve)
                v_seq(lambda e: e.tensor_scalar_mul(lgt_sb[:, :], lgt_sb[:, :],
                                                    smalls[:, 10:11]))
                v_seq(lambda e: e.reduce_max(smalls[:, 0:1], lgt_sb[:, :], AX.X, negate=True))
                nm_ve = P.val("ve")
                p_act = s_add(lambda e: e.activation(probs_sb[:, :], lgt_sb[:, :],
                                                     AF.Exp, bias=smalls[:, 0:1]),
                              waits=[("ve", nm_ve)])
                prev_pact = p_act
                v_seq(lambda e: e.reduce_sum(smalls[:, 1:2], probs_sb[:, :], AX.X),
                      waits=[("act", p_act)])
                v_seq(lambda e: e.reciprocal(smalls[:, 2:3], smalls[:, 1:2]))
                v_seq(lambda e: e.tensor_scalar_mul(probs_sb[:, :], probs_sb[:, :],
                                                    smalls[:, 2:3]))
                v_seq(lambda e: e.reduce_max(smalls[:, 3:4], probs_sb[:, :], AX.X))
                v_seq(lambda e: e.tensor_single_scalar(eq_sb[:, :], probs_sb[:, :],
                                                       smalls[:, 3:4], ALU.is_ge))
                v_seq(lambda e: e.scalar_tensor_tensor(eq_sb[:, :], eq_sb[:, :], -2.0,
                                                       probs_sb[:, :], ALU.mult, ALU.add))
                v_seq(lambda e: e.reduce_max(smalls[:, 4:5], eq_sb[:, :], AX.X))
                v_seq(lambda e: e.tensor_single_scalar(eq_sb[:, :], probs_sb[:, :],
                                                       smalls[:, 4:5], ALU.is_ge))
                v_seq(lambda e: e.tensor_add(smalls[:, 5:6], smalls[:, 3:4], smalls[:, 4:5]))
                v_seq(lambda e: e.tensor_scalar_add(smalls[:, 5:6], smalls[:, 5:6], 1e-20))
                v_seq(lambda e: e.reciprocal(smalls[:, 6:7], smalls[:, 5:6]))
                cs_ve = v_seq(lambda e: e.scalar_tensor_tensor(
                    comb_sb[:, :], probs_sb[:, :], smalls[:, 6:7], eq_sb[:, :],
                    ALU.mult, ALU.mult))
                bsel2 = (6 if bi == 0 else 2) + tc % 2
                tw_x = BK.ww([bsel2]) + [("ve", cs_ve), ("x", ident_rdy)]
                if "5" in PARANOID[0]:
                    tw_x += [("pe", P.val("pe")), ("ve", P.val("ve")), ("act", P.val("act"))]
                t_add(lambda e, bsel2=bsel2: e.transpose(bk(bsel2, 0, 8, 128), comb_sb[:, :],
                                                         ident_sb[:, :]),
                      waits=tw_x)
                tp_pe = P.val("pe")
                comb_ve = v_add(lambda e, tc=tc, bsel2=bsel2: e.tensor_copy(
                    combT[0:8, tc * 128:(tc + 1) * 128], bk(bsel2, 0, 8, 128)),
                    waits=[("pe", tp_pe), ("x", prev_comb_x)])
                BK.consumed([bsel2], "ve", comb_ve)
            x1st_free = [("pe", P.val("pe")), ("ve", P.val("ve"))]
        for e8 in range(NE):
            rb8, cb8 = (e8 // 3) * 32, (e8 % 3) * 1024
            dx(lambda e, e8=e8, rb8=rb8, cb8=cb8: e.dma_start(
                out=comb_rows[rb8:rb8 + 1, cb8:cb8 + 1024], in_=combT[e8:e8 + 1, :]),
               waits=[("ve", comb_ve), ("pe", hcons_pe)])
        comb_x = P.val("x")
        prev_comb_x = comb_x

        dly = [("x", P.val("x")), ("ve", P.val("ve")), ("act", P.val("act"))] if phase == 7 else []
        for ex in [NE] + list(range(NE)):
            shared = (ex == NE)
            dsrc = d_sdT if shared else d_edT[ex]
            gsrc = d_sgT if shared else d_egT[ex]
            usrc = d_suT if shared else d_euT[ex]
            dw_rdy = None
            for mc in range(22):
                bi2 = mc % 2
                if mc == 2:
                    dw_rdy = dw(lambda e, dsrc=dsrc: e.dma_start(
                        out=dw_sb[:, :, :], in_=dsrc.rearrange("(c p) m -> p c m", p=128)),
                        waits=[("pe", hcons_pe)] + dly + barrier)
                gwx = [guv_prev.get(("g", bi2), ("pe", hcons_pe))] + dly + barrier
                if "2" in PARANOID[0]:
                    gwx += [("pe", P.val("pe")), ("ve", P.val("ve")), ("act", P.val("act"))]
                gr = dw(lambda e, gsrc=gsrc, mc=mc, bi2=bi2: e.dma_start(
                    out=gwc[:, bi2, :, :],
                    in_=gsrc.rearrange("(c p) m -> p c m", p=128)[:, :, mc * 128:(mc + 1) * 128]),
                    waits=gwx)
                uwx = [guv_prev.get(("u", bi2), ("pe", hcons_pe))] + dly + barrier
                if "2" in PARANOID[0]:
                    uwx += [("pe", P.val("pe")), ("ve", P.val("ve")), ("act", P.val("act"))]
                ur = dw(lambda e, usrc=usrc, mc=mc, bi2=bi2: e.dma_start(
                    out=uwc[:, bi2, :, :],
                    in_=usrc.rearrange("(c p) m -> p c m", p=128)[:, :, mc * 128:(mc + 1) * 128]),
                    waits=uwx)
                gb = (0, 1) if bi2 == 0 else (4, 5)
                ub = (2, 3) if bi2 == 0 else (6, 7)
                def g_mms(e, bi2=bi2, gb=gb):
                    last = None
                    for tq in range(2):
                        for fc in range(NFC):
                            last = e.matmul(bk(gb[tq]), gwc[:, bi2, fc, :],
                                            xf[:, fc, tq * 512:(tq + 1) * 512],
                                            start=(fc == 0), stop=(fc == NFC - 1))
                    return last
                t_add(g_mms, waits=BK.ww(list(gb)) + [("w", gr), ("ve", xf_ve)])
                g_pe = P.val("pe")
                guv_prev[("g", bi2)] = ("pe", g_pe)
                def u_mms(e, bi2=bi2, ub=ub):
                    last = None
                    for tq in range(2):
                        for fc in range(NFC):
                            last = e.matmul(bk(ub[tq]), uwc[:, bi2, fc, :],
                                            xf[:, fc, tq * 512:(tq + 1) * 512],
                                            start=(fc == 0), stop=(fc == NFC - 1))
                    return last
                t_add(u_mms, waits=BK.ww(list(ub)) + [("w", ur)])
                u_pe = P.val("pe")
                guv_prev[("u", bi2)] = ("pe", u_pe)
                sg_act = s_add(lambda e, gb=gb: e.activation(
                    sg_sb[:, :], pa[:, gb[0] * 512:gb[0] * 512 + 1024], AF.Silu),
                    waits=[("pe", g_pe), ("ve", h_ve)])
                BK.consumed(list(gb), "act", sg_act)
                h_ve = v_add(lambda e, mc=mc, ub=ub: e.tensor_tensor(
                    h_sb[:, mc, :], sg_sb[:, :], pa[:, ub[0] * 512:ub[0] * 512 + 1024], ALU.mult),
                    waits=[("act", sg_act), ("pe", max(u_pe, hcons_pe))])
                BK.consumed(list(ub), "ve", h_ve)
            ct_ve = 0
            if not shared:
                rbase = (0, 32, 64)[ex // 3]
                ccol = (ex % 3) * 1024
                for tq in range(2):
                    bsel = 2 + tq
                    t_add(lambda e, rbase=rbase, ccol=ccol, tq=tq, bsel=bsel: e.matmul(
                        bk(bsel), ones_f[rbase:rbase + 1, 0:128],
                        comb_rows[rbase:rbase + 1, ccol + tq * 512:ccol + (tq + 1) * 512],
                        start=True, stop=True), waits=BK.ww([bsel]) + [("x", comb_x)])
                    cpe = P.val("pe")
                    ct_ve = v_add(lambda e, tq=tq, bsel=bsel: e.tensor_copy(
                        ctile[:, tq, :], bk(bsel)), waits=[("pe", cpe)])
                    BK.consumed([bsel], "ve", ct_ve)
            for mc8 in range(NFC):
                for tq in range(2):
                    bsel = (mc8 * 2 + tq) % 8
                    def d_mms(e, mc8=mc8, tq=tq, bsel=bsel):
                        last = None
                        for kc in range(22):
                            last = e.matmul(bk(bsel), dw_sb[:, kc, mc8 * 128:(mc8 + 1) * 128],
                                            h_sb[:, kc, tq * 512:(tq + 1) * 512],
                                            start=(kc == 0), stop=(kc % 8 == 7 or kc == 21),
                                            skip_group_check=True)
                        return last
                    t_add(d_mms, waits=BK.ww([bsel]) + [("ve", h_ve), ("w", dw_rdy)])
                    dpe = P.val("pe")
                    if shared:
                        v_add(lambda e, mc8=mc8, tq=tq, bsel=bsel: e.tensor_copy(
                            y_acc[:, mc8, tq * 512:(tq + 1) * 512], bk(bsel)),
                              waits=[("pe", dpe)])
                    else:
                        v_add(lambda e, tq=tq, bsel=bsel: e.tensor_tensor(
                            tmp_e[:, :], bk(bsel), ctile[:, tq, :], ALU.mult),
                              waits=[("pe", dpe), ("ve", ct_ve)])
                        v_add(lambda e, mc8=mc8, tq=tq: e.tensor_add(
                            y_acc[:, mc8, tq * 512:(tq + 1) * 512],
                            y_acc[:, mc8, tq * 512:(tq + 1) * 512], tmp_e[:, :]))
                    BK.consumed([bsel], "ve", P.val("ve"))
            hcons_pe = P.val("pe")
        y_done_ve = P.val("ve")

        if phase in (4, 5, 6, 7) and half == 1:
            dbg_pe, dbg_ve = P.val("pe"), P.val("ve")
            if phase == 6:
                dx(lambda e: e.dma_start(out=out3[:, :, :], in_=x13[:, :, :]),
                   waits=[("pe", dbg_pe), ("ve", dbg_ve)])
            if phase in (4, 7):
                yv = y_acc.rearrange("p a b -> p (a b)").rearrange("p (c d) -> p c d", d=2048)
                xv = xf.bitcast(F32).rearrange("p a b -> p (a b)").rearrange("p (c d) -> p c d", d=2048)
                dx(lambda e: e.dma_start(out=out3[:, 0:4, :], in_=yv[:, :, :]),
                   waits=[("pe", dbg_pe), ("ve", dbg_ve)])
                dx(lambda e: e.dma_start(out=out3[:, 4:6, :], in_=xv[:, :, :]),
                   waits=[("pe", dbg_pe), ("ve", dbg_ve)])
            if phase == 5:
                dx(lambda e: e.dma_start(out=out3[:, 0, 0:1536], in_=comb_rows[:, 0:1536]),
                   waits=[("pe", dbg_pe), ("ve", dbg_ve)])
                dx(lambda e: e.dma_start(out=out3[:, 1, 0:1536], in_=comb_rows[:, 1536:3072]),
                   waits=[("pe", dbg_pe), ("ve", dbg_ve)])
                hv8 = h_sb.bitcast(F32).rearrange("p a b -> p (a b)")
                dx(lambda e: e.dma_start(
                    out=out3[:, 2:6, :].rearrange("p c t -> p (c t)")[:, 0:8192],
                    in_=hv8[:, 0:8192]),
                   waits=[("pe", dbg_pe), ("ve", dbg_ve)]) if False else None
            fin = P.val("x")
            P.add("sync", lambda e: e.nop(), waits=[("x", fin)])
            return _emit(nc, P)
        if phase in (4, 5, 6, 7):
            continue
        x1st_free = [("ve", P.val("ve"))]
        for sub4 in range(4):
            bi = sub4 % 2
            scol = slice(sub4 * SUB, (sub4 + 1) * SUB)
            xr = dx(lambda e, sub4=sub4, hoff=hoff: e.dma_start(
                out=x1st[:, :, :], in_=x13[:, :, hoff + sub4 * SUB:hoff + (sub4 + 1) * SUB]),
                waits=x1st_free)
            sumb, sqb, mb_, rb_ = (0, 1, 2, 3) if bi == 0 else (4, 5, 6, 7)
            wws, wwq = BK.ww([sumb]), BK.ww([sqb])
            cw = []
            for fc in range(NFC):
                sl = fc % 2
                v_add(lambda e, fc=fc, sl=sl, scol=scol: e.tensor_copy(
                    ybc[:, sl, :], y_acc[:, fc, scol]),
                      waits=[("ve", y_done_ve)] + (wws if fc == 0 else
                                                   [("pe", cw[fc - 2])] if fc >= 2 else []))
                bv = P.val("ve")
                t_add(lambda e, fc=fc, sl=sl, sumb=sumb: e.matmul(
                    bk(sumb, 0, 1, SUB), ones_bf[:, 0:1], ybc[:, sl, :],
                    start=(fc == 0), stop=(fc == NFC - 1)), waits=[("ve", bv)])
                cw.append(P.val("pe"))
            sum_pe = P.val("pe")
            cw2 = []
            for fc in range(NFC):
                sl = fc % 2
                v_add(lambda e, fc=fc, sl=sl, scol=scol: e.tensor_mul(
                    ysq[:, sl, :], y_acc[:, fc, scol], y_acc[:, fc, scol]),
                      waits=(wwq if fc == 0 else [("pe", cw2[fc - 2])] if fc >= 2 else []))
                sv = P.val("ve")
                t_add(lambda e, fc=fc, sl=sl, sqb=sqb: e.matmul(
                    bk(sqb, 0, 1, SUB), ones_bf[:, 0:1], ysq[:, sl, :],
                    start=(fc == 0), stop=(fc == NFC - 1)), waits=[("ve", sv)])
                cw2.append(P.val("pe"))
            sq_pe = P.val("pe")
            mu_ve = v_add(lambda e, sumb=sumb: e.tensor_scalar_mul(
                rows[0:1, 0:SUB], bk(sumb, 0, 1, SUB), 1.0 / D),
                waits=[("pe", sum_pe), ("pe", rows_pe["v"])])
            v_add(lambda e, sqb=sqb: e.tensor_scalar_mul(rows[32:33, 0:SUB],
                                                         bk(sqb, 0, 1, SUB), 1.0 / D),
                  waits=[("pe", sq_pe)])
            v_add(lambda e: e.tensor_mul(rows[32:33, SUB:2 * SUB],
                                         rows[0:1, 0:SUB], rows[0:1, 0:SUB]))
            var_ve = v_add(lambda e: e.tensor_sub(rows[64:65, 0:SUB], rows[32:33, 0:SUB],
                                                  rows[32:33, SUB:2 * SUB]))
            s_add(lambda e: e.activation(rows[64:65, SUB:2 * SUB], rows[64:65, 0:SUB],
                                         AF.Ln, bias=EPS), waits=[("ve", var_ve)])
            r_act = s_add(lambda e: e.activation(rows[64:65, 0:SUB], rows[64:65, SUB:2 * SUB],
                                                 AF.Exp, scale=-0.5))
            t_add(lambda e, mb_=mb_: e.matmul(bk(mb_, w=SUB), ones_f[0:1, 0:128],
                                              rows[0:1, 0:SUB], start=True, stop=True),
                  waits=BK.ww([mb_]) + [("ve", mu_ve)])
            mpe = P.val("pe")
            t_add(lambda e, rb_=rb_: e.matmul(bk(rb_, w=SUB), ones_f[64:65, 0:128],
                                              rows[64:65, 0:SUB], start=True, stop=True),
                  waits=BK.ww([rb_]) + [("act", r_act)])
            rpe = P.val("pe")
            rows_pe["v"] = rpe
            for fc in range(NFC):
                v_add(lambda e, fc=fc, scol=scol, mb_=mb_: e.tensor_sub(
                    tmp_e[:, 0:SUB], y_acc[:, fc, scol], bk(mb_, w=SUB)),
                      waits=[("pe", max(mpe, rpe)), ("x", xr)])
                v_add(lambda e, rb_=rb_: e.tensor_tensor(
                    tmp_e[:, 0:SUB], tmp_e[:, 0:SUB], bk(rb_, w=SUB), ALU.mult))
                v_add(lambda e, fc=fc: e.tensor_add(x1st[:, fc, :], tmp_e[:, 0:SUB],
                                                    x1st[:, fc, :]))
            x2v = P.val("ve")
            BK.consumed([sumb, sqb, mb_, rb_], "ve", x2v)
            ov = dx(lambda e, sub4=sub4, hoff=hoff: e.dma_start(
                out=out3[:, :, hoff + sub4 * SUB:hoff + (sub4 + 1) * SUB], in_=x1st[:, :, :]),
                waits=[("ve", x2v)])
            x1st_free = [("x", ov)]
            out_parts.append(ov)

    fin = P.val("x")
    P.add("sync", lambda e: e.nop(), waits=[("x", fin)])
    return _emit(nc, P)


def _emit(nc, P):
    with nc.semaphore("s_x") as s_x, nc.semaphore("s_w") as s_w, \
         nc.semaphore("s_pe") as s_pe, nc.semaphore("s_act") as s_act, \
         nc.semaphore("s_ve") as s_ve, nc.Block() as blk:
        sem_map = {"x": s_x, "w": s_w, "pe": s_pe, "act": s_act, "ve": s_ve}

        def emit(eng_name):
            def body(eng):
                for wl, fn, inc in P.steps[eng_name]:
                    for s, v in wl:
                        eng.wait_ge(sem_map[s], v)
                    inst = fn(eng)
                    if inc is not None:
                        inst.then_inc(sem_map[inc[0]], inc[1])
            return body

        blk.sync(emit("sync"))
        blk.gpsimd(emit("gpsimd"))
        blk.tensor(emit("tensor"))
        blk.vector(emit("vector"))
        blk.scalar(emit("scalar"))
    return nc


def host_inputs(inputs):
    g = lambda k: np.ascontiguousarray(np.asarray(inputs[k], dtype=np.float32))
    x = g("L0_attn_x")
    xfm = np.ascontiguousarray(x.transpose(0, 2, 1))
    wq, wk, wv, wu = g("L0_attn_w_q"), g("L0_attn_w_k"), g("L0_attn_w_v"), g("L0_attn_w_u")
    bu = g("L0_attn_b_u").reshape(1, D)
    gwT = np.ascontiguousarray(g("L0_ffn_gate_w").T)
    egT = np.ascontiguousarray(g("L0_ffn_expert_gate").transpose(0, 2, 1))
    euT = np.ascontiguousarray(g("L0_ffn_expert_up").transpose(0, 2, 1))
    edT = np.ascontiguousarray(g("L0_ffn_expert_down").transpose(0, 2, 1))
    sgT = np.ascontiguousarray(g("L0_ffn_shared_gate").T)
    suT = np.ascontiguousarray(g("L0_ffn_shared_up").T)
    sdT = np.ascontiguousarray(g("L0_ffn_shared_down").T)
    ident = np.eye(128, dtype=np.float32)

    in_maps, metas = [], []
    for c in range(NCORES):
        seq, r = c // 4, c % 4
        tiles = [4 * (e - 1) + r for e in range(16, 0, -1)]
        cols = np.concatenate([np.arange(i * 128, (i + 1) * 128) for i in tiles])
        metas.append((seq, cols))
        xown = np.ascontiguousarray(xfm[seq][:, cols])
        p = np.arange(128)[None, :, None]
        cq = np.arange(128)[None, None, :]
        kcc = np.arange(4)[:, None, None]
        m = (r * 128 + cq >= kcc * 128 + p).astype(np.float32)
        mask = np.ascontiguousarray(m.transpose(1, 0, 2).reshape(128, 512)).astype(
            ml_dtypes.bfloat16)
        in_maps.append({
            "xseq": xfm[seq], "xown": xown, "mask": mask, "ident": ident,
            "wq": wq, "wk": wk, "wv": wv, "wu": wu, "bu": bu, "gwT": gwT,
            "egT": egT, "euT": euT, "edT": edT, "sgT": sgT, "suT": suT, "sdT": sdT,
        })
    return in_maps, metas


def run_device(inputs, phase=2, nb=NB):
    in_maps, metas = host_inputs(inputs)
    nc = build_nc(phase=phase, nb=nb)
    res = run_bass_kernel_spmd(nc, in_maps, core_ids=list(range(NCORES)))
    out = np.empty((B, S, D), dtype=np.float32)
    for c in range(NCORES):
        seq, cols = metas[c]
        out[seq, cols, :] = res.results[c]["xout"].T
    return out


def kernel(**inputs):
    return run_device(inputs, phase=2, nb=NB)


# version 06 (final)
# speedup vs baseline: 1.3087x; 1.3087x over previous
import numpy as np
import ml_dtypes
import concourse.bass as bass
import concourse.mybir as mybir
from concourse.bass_utils import run_bass_kernel_spmd

F32 = mybir.dt.float32
BF16 = mybir.dt.bfloat16
AF = mybir.ActivationFunctionType
ALU = mybir.AluOpType
AX = mybir.AxisListType

B, S, D = 2, 8192, 1024
H = 16
DFF, NE = 2816, 8
NB, KB = 16, 512
TOK = 2048
NCORES = 8
NFC = 8
SUB = 256
EPS = 1e-5

ENGINES = ("tensor", "vector", "scalar", "sync", "gpsimd")


class Prog:

    def __init__(self):
        self.steps = {e: [] for e in ENGINES}
        self.count = {}
        self.hwm = {e: {} for e in ENGINES}
        self.own = {"tensor": "pe", "vector": "ve", "scalar": "act"}

    def val(self, sem):
        return self.count.get(sem, 0)

    def add(self, eng, fn, waits=(), inc=None, self_ok=False):
        wl = []
        for s, v in waits:
            assert v <= self.count.get(s, 0), ("forward wait", eng, s, v, self.count.get(s, 0))
            if v <= 0 or (self.own.get(eng) == s and not self_ok):
                continue
            if self.hwm[eng].get(s, -1) >= v:
                continue
            self.hwm[eng][s] = v
            wl.append((s, v))
        if inc is not None:
            s, k = inc
            self.count[s] = self.count.get(s, 0) + k
        self.steps[eng].append((wl, fn, inc))


class Banks:

    def __init__(self):
        self.cons = [dict() for _ in range(8)]
        self.last_grp_bank = -1

    def ww(self, banks):
        w = {}
        for b in banks:
            for s, v in self.cons[b].items():
                w[s] = max(w.get(s, 0), v)
            self.cons[b] = {}
        return list(w.items())

    def consumed(self, banks, sem, val):
        for b in banks:
            self.cons[b][sem] = max(self.cons[b].get(sem, 0), val)


PARANOID = [""]


def build_nc(phase=2, nb=NB):
    nc = bass.Bass("TRN2", target_bir_lowering=False)

    d_xseq = nc.dram_tensor("xseq", [D, S], F32, kind="ExternalInput").ap()
    d_xown = nc.dram_tensor("xown", [D, TOK], F32, kind="ExternalInput").ap()
    d_mask = nc.dram_tensor("mask", [128, 512], BF16, kind="ExternalInput").ap()
    d_ident = nc.dram_tensor("ident", [128, 128], F32, kind="ExternalInput").ap()
    d_wq = nc.dram_tensor("wq", [D, D], F32, kind="ExternalInput").ap()
    d_wk = nc.dram_tensor("wk", [D, D], F32, kind="ExternalInput").ap()
    d_wv = nc.dram_tensor("wv", [D, D], F32, kind="ExternalInput").ap()
    d_wu = nc.dram_tensor("wu", [D, D], F32, kind="ExternalInput").ap()
    d_bu = nc.dram_tensor("bu", [1, D], F32, kind="ExternalInput").ap()
    d_gwT = nc.dram_tensor("gwT", [D, NE], F32, kind="ExternalInput").ap()
    d_egT = nc.dram_tensor("egT", [NE, D, DFF], F32, kind="ExternalInput").ap()
    d_euT = nc.dram_tensor("euT", [NE, D, DFF], F32, kind="ExternalInput").ap()
    d_edT = nc.dram_tensor("edT", [NE, DFF, D], F32, kind="ExternalInput").ap()
    d_sgT = nc.dram_tensor("sgT", [D, DFF], F32, kind="ExternalInput").ap()
    d_suT = nc.dram_tensor("suT", [D, DFF], F32, kind="ExternalInput").ap()
    d_sdT = nc.dram_tensor("sdT", [DFF, D], F32, kind="ExternalInput").ap()
    d_out = nc.dram_tensor("xout", [D, TOK], F32, kind="ExternalOutput").ap()

    d_x1 = nc.dram_tensor("x1sp", [128, NFC, TOK], F32, kind="Internal").ap()
    d_l = nc.dram_tensor("ldram", [H, NB, TOK], F32, kind="Internal").ap()
    d_rl = nc.dram_tensor("rldram", [H, TOK], F32, kind="Internal").ap()

    xseq3 = d_xseq.rearrange("(c p) t -> p c t", p=128)
    xown3 = d_xown.rearrange("(c p) t -> p c t", p=128)
    out3 = d_out.rearrange("(c p) t -> p c t", p=128)
    x13 = d_x1
    wq3 = d_wq.rearrange("(c p) m -> p c m", p=128)
    wk3 = d_wk.rearrange("(c p) m -> p c m", p=128)
    wv3 = d_wv.rearrange("(c p) m -> p c m", p=128)
    wu3 = d_wu.rearrange("(c p) m -> p c m", p=128)

    cur = [16832]
    offs = {}

    def sb(name, shape, dt, at=None, align=64):
        if at is None:
            cur[0] = (cur[0] + align - 1) // align * align
            at = cur[0]
            n = 1
            for s_ in shape[1:]:
                n *= s_
            cur[0] += n * (2 if dt == BF16 else 4)
        h = nc.alloc_sbuf_tensor_at(name, shape, dt, offset=at)
        offs[id(h)] = at
        return h

    def off(h):
        return offs[id(h)]

    const_sb = sb("const_sb", [128, 4], F32)
    nc.const_aps.register(F32, EPS, const_sb[:, 0:1])
    nc.const_aps.register(F32, -0.5, const_sb[:, 1:2])
    ones_bf = sb("ones_bf", [128, 128], BF16)
    ones_f = sb("ones_f", [128, 128], F32)
    sel_f = sb("sel_f", [128, 256], F32)
    bu_col = sb("bu_col", [128, 8], F32)
    mask_sb = sb("mask_sb", [128, 512], BF16)
    rows = sb("rows", [128, 1024], F32)
    rows2 = sb("rows2", [128, 1024], F32)
    ident_sb = sb("ident_sb", [128, 128], F32)
    base_arena = cur[0]

    cur[0] = base_arena
    Q = sb("Q", [128, NFC, TOK], BF16)
    O_sb = sb("O_sb", [128, NFC, TOK], F32)
    wkv = sb("wkv", [128, 2, NFC, 1024], BF16)
    x_blk = sb("x_blk", [128, 2, NFC, SUB], F32)
    xb = sb("xb", [128, 1, NFC, SUB], BF16)
    lnz = sb("lnz", [128, NFC, KB], BF16)
    xsqc = sb("xsqc", [128, 4, SUB], BF16)
    K_fm = sb("K_fm", [128, NFC, KB], BF16)
    V_tm = sb("V_tm", [128, 4, H * 65], BF16)
    PT = sb("PT", [128, 2, 3, 512], BF16)
    mu_b = sb("mu_b", [128, KB], F32)
    rstd_b = sb("rstd_b", [128, KB], F32)
    rstd_c = sb("rstd_c", [128, 4], F32)
    l_stage = sb("l_stage", [128, TOK], F32)
    attn_end = cur[0]

    rstdq_b = sb("rstdq_b", [128, KB], F32, at=off(K_fm))
    l16a = sb("l16a", [16, TOK], F32, at=off(x_blk))
    l16b = sb("l16b", [16, TOK], F32, at=off(lnz))
    y_att = sb("y_att", [128, 2, NFC, SUB], F32, at=off(x_blk))
    O_bf = sb("O_bf", [128, 2, NFC, SUB], BF16, at=off(V_tm))
    ep_t1 = sb("ep_t1", [128, SUB], F32, at=off(PT))
    ep_t2 = sb("ep_t2", [128, SUB], F32, at=off(PT) + 1024)
    ep_bc = sb("ep_bc", [128, 2, SUB], BF16, at=off(PT) + 2048)
    ep_sq = sb("ep_sq", [128, 2, SUB], BF16, at=off(PT) + 3072)
    ep_x = sb("ep_x", [128, NFC, SUB], F32, at=off(l_stage))
    zeros4k = sb("zeros4k", [128, 4096], F32, at=off(O_sb))
    rl_st = sb("rl_st", [128, 4, 512], F32, at=off(Q))

    cur[0] = base_arena
    xf = sb("xf", [128, NFC, 1024], BF16)
    h_sb = sb("h_sb", [128, 22, 1024], BF16)
    dw_sb = sb("dw_sb", [128, 22, 1024], BF16)
    y_acc = sb("y_acc", [128, NFC, 1024], F32)
    gwc = sb("gwc", [128, 2, NFC, 128], BF16)
    uwc = sb("uwc", [128, 2, NFC, 128], BF16)
    sg_sb = sb("sg_sb", [128, 1024], BF16)
    tmp_e = sb("tmp_e", [128, 512], F32)
    comb_rows = sb("comb_rows", [128, 3072], F32)
    combT = sb("combT", [8, 1024], F32)
    comb_sb = sb("comb_sb", [128, 8], F32)
    probs_sb = sb("probs_sb", [128, 8], F32)
    eq_sb = sb("eq_sb", [128, 8], F32)
    smalls = sb("smalls", [128, 16], F32)
    gwT32 = sb("gwT32", [128, NFC, 8], F32)
    gsum_b = sb("gsum_b", [128, 8], F32)
    lgt_sb = sb("lgt_sb", [128, 8], F32)
    x1st = sb("x1st", [128, NFC, SUB], F32)
    ybc = sb("ybc", [128, 2, SUB], BF16)
    ysq = sb("ysq", [128, 2, SUB], BF16)
    ctile = sb("ctile", [128, 2, 512], F32)
    ffn_end = cur[0]
    assert max(attn_end, ffn_end) <= 228500, (attn_end, ffn_end)

    pa = nc.alloc_psum_tensor("pa", [128, 4096], F32)

    def bk(b, p0=0, p1=128, w=512):
        return pa[p0:p1, b * 512:b * 512 + w]

    P = Prog()
    BK = Banks()
    rows_pe = {"v": 0}

    def _pw(waits, eng):
        if eng not in PARANOID[0]:
            return waits
        return list(waits) + [(sname, P.val(sname)) for sname in ("pe", "ve", "act", "x", "w")]

    def t_add(fn, waits=(), inc=("pe", 1)):
        P.add("tensor", fn, _pw(waits, "t"), inc)
        return P.val("pe")

    def v_add(fn, waits=(), inc=("ve", 1)):
        P.add("vector", fn, _pw(waits, "v"), inc)
        return P.val("ve")

    def v_seq(fn, waits=(), inc=("ve", 1)):
        P.add("vector", fn, list(waits) + [("ve", P.val("ve"))], inc, self_ok=True)
        return P.val("ve")

    def s_add(fn, waits=(), inc=("act", 1)):
        P.add("scalar", fn, _pw(waits, "s"), inc)
        return P.val("act")

    def dx(fn, waits=()):
        P.add("sync", fn, _pw(waits, "x"), ("x", 16))
        return P.val("x")

    def dw(fn, waits=()):
        P.add("gpsimd", fn, _pw(waits, "w"), ("w", 16))
        return P.val("w")

    mask_rdy = dx(lambda e: e.dma_start(out=mask_sb[:, :], in_=d_mask))
    ident_rdy = dx(lambda e: e.dma_start(out=ident_sb[:, :], in_=d_ident))
    burow_rdy = dx(lambda e: e.dma_start(out=rows2[0:1, 0:1024], in_=d_bu))

    v_add(lambda e: e.memset(const_sb[:, 0:1], EPS))
    v_add(lambda e: e.memset(const_sb[:, 1:2], -0.5))
    v_add(lambda e: e.memset(ones_bf[:, :], 1.0))
    v_add(lambda e: e.memset(ones_f[:, :], 1.0))
    v_add(lambda e: e.memset(sel_f[:, :], 0.0))
    for r in (0, 32, 64):
        v_add(lambda e, r=r: e.memset(sel_f[r:r + 1, 0:64], 1.0))
        v_add(lambda e, r=r: e.memset(sel_f[r:r + 1, 192:256], 1.0))
    init_ve = P.val("ve")

    for of in range(8):
        bsel = of % 2
        t_add(lambda e, of=of, bsel=bsel: e.matmul(
            bk(bsel, w=1), rows2[0:1, of * 128:(of + 1) * 128], ones_f[0:1, 0:1],
            start=True, stop=True),
            waits=BK.ww([bsel]) + [("x", burow_rdy), ("ve", init_ve)])
        pv = P.val("pe")
        v_add(lambda e, of=of, bsel=bsel: e.tensor_copy(bu_col[:, of:of + 1], bk(bsel, w=1)),
              waits=[("pe", pv)])
        BK.consumed([bsel], "ve", P.val("ve"))
    bu_ve = P.val("ve")

    v_add(lambda e: e.memset(zeros4k[:, :], 0.0))
    zv = P.val("ve")
    ld_flat = d_l.rearrange("h j q -> (h j q)").rearrange("(p n) -> p n", p=128)
    ldzero_x = dx(lambda e: e.dma_start(out=ld_flat[:, :], in_=zeros4k[:, :]),
                  waits=[("ve", zv)])

    wq_rdy = dw(lambda e: e.dma_start(out=wkv[:, 0, :, :], in_=wq3))

    v_add(lambda e: e.memset(V_tm[:, :, :], 0.0))
    vt4 = V_tm.rearrange("p a (h c) -> p a h c", c=65)
    v_add(lambda e: e.memset(vt4[:, :, :, 64:65], 1.0))
    vones_ve = P.val("ve")

    xb_consumed = {0: ("pe", 0)}
    xblk_consumed = {0: ("ve", 0), 1: ("ve", 0)}

    def ln_stats(bi, sub_cols, banks, xb_rdy):
        sb_, qb_, mb_, rb_ = banks
        def sum_grp(e):
            last = None
            for fc in range(NFC):
                last = e.matmul(bk(sb_, 0, 1, SUB), ones_bf[:, 0:1], xb[:, 0, fc, :],
                                start=(fc == 0), stop=(fc == NFC - 1))
            return last
        t_add(sum_grp, waits=BK.ww([sb_]) + [("w", xb_rdy)])
        sum_pe = P.val("pe")
        wwq = BK.ww([qb_])
        sqmm_pe = []
        for fc in range(NFC):
            slot = bi * 2 + fc % 2
            wv = [("w", xb_rdy)] + (wwq if fc == 0 else
                                    [("pe", sqmm_pe[fc - 2])] if fc >= 2 else [])
            sv = v_add(lambda e, fc=fc, slot=slot: e.tensor_mul(
                xsqc[:, slot, :], xb[:, 0, fc, :], xb[:, 0, fc, :]), waits=wv)
            t_add(lambda e, fc=fc, slot=slot: e.matmul(
                bk(qb_, 0, 1, SUB), ones_bf[:, 0:1], xsqc[:, slot, :],
                start=(fc == 0), stop=(fc == NFC - 1)), waits=[("ve", sv)])
            sqmm_pe.append(P.val("pe"))
        sq_pe = P.val("pe")
        mu_ve = v_add(lambda e: e.tensor_scalar_mul(rows[0:1, 0:SUB], bk(sb_, 0, 1, SUB), 1.0 / D),
                      waits=[("pe", sum_pe), ("pe", rows_pe["v"])])
        v_add(lambda e: e.tensor_scalar_mul(rows[32:33, 0:SUB], bk(qb_, 0, 1, SUB), 1.0 / D),
              waits=[("pe", sq_pe)])
        v_add(lambda e: e.tensor_mul(rows[32:33, SUB:2 * SUB], rows[0:1, 0:SUB], rows[0:1, 0:SUB]))
        var_ve = v_add(lambda e: e.tensor_sub(rows[64:65, 0:SUB], rows[32:33, 0:SUB],
                                              rows[32:33, SUB:2 * SUB]))
        s_add(lambda e: e.activation(rows[64:65, SUB:2 * SUB], rows[64:65, 0:SUB], AF.Ln, bias=EPS),
              waits=[("ve", var_ve)])
        rstd_act = s_add(lambda e: e.activation(rows[64:65, 0:SUB], rows[64:65, SUB:2 * SUB],
                                                AF.Exp, scale=-0.5))
        t_add(lambda e: e.matmul(bk(mb_, w=SUB), ones_f[0:1, 0:128], rows[0:1, 0:SUB],
                                 start=True, stop=True), waits=BK.ww([mb_]) + [("ve", mu_ve)])
        mub_pe = P.val("pe")
        t_add(lambda e: e.matmul(bk(rb_, w=SUB), ones_f[64:65, 0:128], rows[64:65, 0:SUB],
                                 start=True, stop=True), waits=BK.ww([rb_]) + [("act", rstd_act)])
        rb_pe = P.val("pe")
        rows_pe["v"] = rb_pe
        v_add(lambda e: e.tensor_copy(mu_b[:, sub_cols], bk(mb_, w=SUB)), waits=[("pe", mub_pe)])
        BK.consumed([mb_], "ve", P.val("ve"))
        v_add(lambda e: e.tensor_copy(rstd_b[:, sub_cols], bk(rb_, w=SUB)), waits=[("pe", rb_pe)])
        BK.consumed([rb_, sb_, qb_], "ve", P.val("ve"))
        return rstd_act

    def ln_z(bi, sub_cols, xblk_rdy):
        for fc in range(NFC):
            v_add(lambda e, fc=fc: e.tensor_sub(lnz[:, fc, sub_cols], x_blk[:, bi, fc, :],
                                                mu_b[:, sub_cols]),
                  waits=[("x", xblk_rdy)])
        return P.val("ve")

    def proj_grp(e, wi, ofc, width, bank):
        last = None
        for fc in range(NFC):
            last = e.matmul(bk(bank, w=width), wkv[:, wi, fc, ofc * 128:(ofc + 1) * 128],
                            lnz[:, fc, 0:width], start=(fc == 0), stop=(fc == NFC - 1))
        return last

    qproj_done_pe = 0
    for blk in range(4):
        z_end = 0
        for half in range(2):
            subi = blk * 2 + half
            bi = subi % 2
            cols = slice(half * SUB, (half + 1) * SUB)
            xr = dx(lambda e, subi=subi, bi=bi: e.dma_start(
                out=x_blk[:, bi, :, :], in_=xown3[:, :, subi * SUB:(subi + 1) * SUB]),
                waits=[xblk_consumed[bi]])
            br = dw(lambda e, subi=subi, bi=bi: e.dma_start(
                out=xb[:, 0, :, :], in_=xown3[:, :, subi * SUB:(subi + 1) * SUB]),
                waits=[xb_consumed[0]])
            banks = (0, 1, 2, 3) if bi == 0 else (4, 5, 6, 7)
            ln_stats(bi, cols, banks, br)
            xb_consumed[0] = ("pe", P.val("pe"))
            rq_ve = v_add(lambda e: e.tensor_scalar_mul(rows[32:33, 0:SUB],
                                                        rows[64:65, 0:SUB], 1.0 / 64))
            t_add(lambda e, banks=banks: e.matmul(
                bk(banks[2], w=SUB), ones_f[32:33, 0:128], rows[32:33, 0:SUB],
                start=True, stop=True), waits=BK.ww([banks[2]]) + [("ve", rq_ve)])
            rq_pe = P.val("pe")
            rows_pe["v"] = rq_pe
            v_add(lambda e, cols=cols, banks=banks: e.tensor_copy(
                rstdq_b[:, cols], bk(banks[2], w=SUB)), waits=[("pe", rq_pe)])
            BK.consumed([banks[2]], "ve", P.val("ve"))
            zv = ln_z(bi, cols, xr)
            xblk_consumed[bi] = ("ve", zv)
            z_end = max(z_end, zv)
        for qf in range(NFC):
            bsel = (qf % 4) * 2
            t_add(lambda e, qf=qf, bsel=bsel: proj_grp(e, 0, qf, KB, bsel),
                  waits=BK.ww([bsel]) + [("ve", z_end), ("w", wq_rdy)])
            ppe = P.val("pe")
            v_add(lambda e, qf=qf, blk=blk, bsel=bsel: e.tensor_tensor(
                Q[:, qf, blk * KB:(blk + 1) * KB], bk(bsel), rstdq_b[:, :], ALU.mult),
                  waits=[("pe", ppe)])
            BK.consumed([bsel], "ve", P.val("ve"))
        qproj_done_pe = P.val("pe")
    q_ready_ve = P.val("ve")

    wk_rdy = dw(lambda e: e.dma_start(out=wkv[:, 0, :, :], in_=wk3), waits=[("pe", qproj_done_pe)])
    wv_rdy = dw(lambda e: e.dma_start(out=wkv[:, 1, :, :], in_=wv3))

    TGRP = [[(0, 0), (0, 1)], [(1, 0), (1, 1)], [(2, 0), (2, 1)], [(3, 0), (3, 1)]]
    state = {
        "kfm_pe": 0, "vtm_pe": 0, "o_ve": 0,
        "lslot_free": [("x", 0), ("x", 0), ("x", 0)], "lslot": 0,
        "pt_free": [("pe", 0), ("pe", 0)],
    }

    def kv_phase(j):
        z_end = 0
        for half in range(2):
            bi = half
            cols = slice(half * SUB, (half + 1) * SUB)
            tok0 = j * KB + half * SUB
            xr = dx(lambda e, bi=bi, tok0=tok0: e.dma_start(
                out=x_blk[:, bi, :, :], in_=xseq3[:, :, tok0:tok0 + SUB]),
                waits=[xblk_consumed[bi]])
            br = dw(lambda e, bi=bi, tok0=tok0: e.dma_start(
                out=xb[:, 0, :, :], in_=xseq3[:, :, tok0:tok0 + SUB]),
                waits=[xb_consumed[0]])
            banks = (0, 1, 2, 3) if bi == 0 else (4, 5, 6, 7)
            rstd_act = ln_stats(bi, cols, banks, br)
            xb_consumed[0] = ("pe", P.val("pe"))
            for k2 in range(2):
                kcc = half * 2 + k2
                t_add(lambda e, k2=k2, banks=banks: e.matmul(
                    bk(banks[2], w=1), rows[64:65, k2 * 128:(k2 + 1) * 128], ones_f[64:65, 0:1],
                    start=True, stop=True),
                    waits=BK.ww([banks[2]]) + [("act", rstd_act)])
                cpe = P.val("pe")
                rows_pe["v"] = cpe
                v_add(lambda e, kcc=kcc, banks=banks: e.tensor_copy(
                    rstd_c[:, kcc:kcc + 1], bk(banks[2], w=1)), waits=[("pe", cpe)])
                BK.consumed([banks[2]], "ve", P.val("ve"))
            zv = ln_z(bi, cols, xr)
            xblk_consumed[bi] = ("ve", zv)
            z_end = max(z_end, zv)
        for kf in range(NFC):
            bsel = (kf % 4) * 2
            t_add(lambda e, kf=kf, bsel=bsel: proj_grp(e, 0, kf, KB, bsel),
                  waits=BK.ww([bsel]) + [("ve", z_end), ("w", wk_rdy), ("pe", state["kfm_pe"])])
            ppe = P.val("pe")
            v_add(lambda e, kf=kf, bsel=bsel: e.tensor_tensor(
                K_fm[:, kf, :], bk(bsel), rstd_b[:, :], ALU.mult), waits=[("pe", ppe)])
            BK.consumed([bsel], "ve", P.val("ve"))
        for tk in range(4):
            for vg in range(2):
                bsel = ((tk * 2 + vg) % 4) * 2 + 1
                def vmm(e, tk=tk, vg=vg, bsel=bsel):
                    last = None
                    for fc in range(NFC):
                        last = e.matmul(bk(bsel), lnz[:, fc, tk * 128:(tk + 1) * 128],
                                        wkv[:, 1, fc, vg * 512:(vg + 1) * 512],
                                        start=(fc == 0), stop=(fc == NFC - 1))
                    return last
                t_add(vmm, waits=BK.ww([bsel]) + [("ve", max(z_end, vones_ve)),
                                                  ("w", wv_rdy), ("pe", state["vtm_pe"])])
                ppe = P.val("pe")
                v_add(lambda e, tk=tk, vg=vg, bsel=bsel: e.tensor_scalar(
                    vt4[:, tk, vg * 8:(vg + 1) * 8, 0:64],
                    bk(bsel).rearrange("p (h c) -> p h c", c=64),
                    rstd_c[:, tk:tk + 1], None, ALU.mult), waits=[("pe", ppe)])
                BK.consumed([bsel], "ve", P.val("ve"))
        return P.val("ve")

    def heads_phase(j, kv_ve):
        prefix = TOK - 128 * j
        widths = []
        off = 0
        while off < prefix:
            w = min(512, prefix - off)
            widths.append((off, w))
            off += w
        qn_ctr = [0]
        pending = [None]

        def flush():
            if pending[0] is not None:
                pending[0]()
                pending[0] = None

        for hp in range(NFC):
            hA, hB = 2 * hp, 2 * hp + 1
            for qi, (qoff, W) in enumerate(widths):
                is_last = (qoff + W == prefix)
                is_hp_last = (qi == len(widths) - 1)
                avb6 = 4 if qn_ctr[0] % 2 == 0 else 6
                qn_ctr[0] += 1
                for gi, grp in enumerate(TGRP):
                    sbanks = (0, 1) if gi % 2 == 0 else (2, 3)
                    pt_i = gi % 2
                    def s_mms(e, grp=grp, sbanks=sbanks, hp=hp, qoff=qoff, W=W):
                        last = None
                        for si, (kcc, hb) in enumerate(grp):
                            p0 = hb * 64
                            last = e.matmul(
                                bk(sbanks[si], w=W),
                                K_fm[p0:p0 + 64, hp, kcc * 128:(kcc + 1) * 128],
                                Q[p0:p0 + 64, hp, qoff:qoff + W],
                                start=True, stop=True, tile_position=(p0, 0))
                        return last
                    t_add(s_mms, waits=BK.ww(list(sbanks)) +
                          [("ve", max(kv_ve, q_ready_ve)), state["pt_free"][pt_i]])
                    s_pe = P.val("pe")
                    def expf(e, sbanks=sbanks, pt_i=pt_i, W=W):
                        src = pa[:, sbanks[0] * 512:sbanks[0] * 512 + 2 * 512]
                        src = src.rearrange("p (n w) -> p n w", w=512)[:, :, 0:W]
                        return e.activation(PT[:, pt_i, 0:2, 0:W], src, AF.Exp)
                    e_act = s_add(expf, waits=[("pe", s_pe)])
                    BK.consumed(list(sbanks), "act", e_act)
                    m_ve = None
                    if is_last:
                        for si, (kcc, hb) in enumerate(grp):
                            v_add(lambda e, si=si, kcc=kcc, pt_i=pt_i, W=W: e.tensor_mul(
                                PT[:, pt_i, si, W - 128:W], PT[:, pt_i, si, W - 128:W],
                                mask_sb[:, kcc * 128:(kcc + 1) * 128]),
                                  waits=[("act", e_act), ("x", mask_rdy)])
                        m_ve = P.val("ve")
                    flush()

                    def av_emit(e_act=e_act, m_ve=m_ve, grp=grp, pt_i=pt_i, hp=hp,
                                hA=hA, hB=hB, qoff=qoff, W=W, gi=gi, avb6=avb6,
                                is_hp_last=is_hp_last, j=j, prefix=prefix):
                        def av_mms(e):
                            last = None
                            for si, (kcc, hb) in enumerate(grp):
                                hh = 2 * hp + hb
                                last = e.matmul(
                                    bk(avb6 + hb, 0, 65, W),
                                    V_tm[:, kcc, hh * 65:hh * 65 + 65],
                                    PT[:, pt_i, si, 0:W],
                                    start=(gi == 0), stop=(gi == 3))
                            return last
                        avw = ([("ve", m_ve)] if m_ve else [("act", e_act)]) + \
                              (BK.ww([avb6, avb6 + 1]) if gi == 0 else [])
                        t_add(av_mms, waits=avw)
                        av_pe = P.val("pe")
                        state["pt_free"][pt_i] = ("pe", av_pe)
                        if gi != 3:
                            return
                        ow = [("pe", av_pe)] + ([("x", ldzero_x)] if j == 0 else [])
                        if j == 0:
                            v_add(lambda e: e.tensor_copy(
                                O_sb[0:64, hp, qoff:qoff + W], bk(avb6, 0, 64, W)), waits=ow)
                            v_add(lambda e: e.tensor_copy(
                                O_sb[64:128, hp, qoff:qoff + W], bk(avb6 + 1, 0, 64, W)))
                        else:
                            v_add(lambda e: e.tensor_add(
                                O_sb[0:64, hp, qoff:qoff + W], O_sb[0:64, hp, qoff:qoff + W],
                                bk(avb6, 0, 64, W)), waits=ow)
                            v_add(lambda e: e.tensor_add(
                                O_sb[64:128, hp, qoff:qoff + W], O_sb[64:128, hp, qoff:qoff + W],
                                bk(avb6 + 1, 0, 64, W)))
                        sA = state["lslot"] % 3
                        sB = (state["lslot"] + 1) % 3
                        v_add(lambda e: e.tensor_copy(
                            l_stage[sA * 32:sA * 32 + 1, qoff:qoff + W], bk(avb6, 64, 65, W)),
                              waits=[state["lslot_free"][sA]])
                        v_add(lambda e: e.tensor_copy(
                            l_stage[sB * 32:sB * 32 + 1, qoff:qoff + W], bk(avb6 + 1, 64, 65, W)),
                              waits=[state["lslot_free"][sB]])
                        state["o_ve"] = P.val("ve")
                        BK.consumed([avb6, avb6 + 1], "ve", state["o_ve"])
                        if not is_hp_last:
                            return
                        state["lslot"] += 2
                        dx(lambda e: e.dma_start(
                            out=d_l[hA, j:j + 1, 0:prefix],
                            in_=l_stage[sA * 32:sA * 32 + 1, 0:prefix]),
                           waits=[("ve", state["o_ve"]), ("x", ldzero_x)])
                        state["lslot_free"][sA] = ("x", P.val("x"))
                        dx(lambda e: e.dma_start(
                            out=d_l[hB, j:j + 1, 0:prefix],
                            in_=l_stage[sB * 32:sB * 32 + 1, 0:prefix]))
                        state["lslot_free"][sB] = ("x", P.val("x"))
                    pending[0] = av_emit
        flush()
        state["kfm_pe"] = P.val("pe")
        state["vtm_pe"] = P.val("pe")

    for j in range(nb):
        kvv = kv_phase(j)
        heads_phase(j, kvv)

    wu_rdy = dw(lambda e: e.dma_start(out=wkv[:, 0, :, :], in_=wu3),
                waits=[("pe", state["kfm_pe"])])

    l16 = [l16a, l16b]
    l16_free = [("pe", 0), ("pe", 0)]
    spill_hist = []
    rl_head_x = {}
    for hh in range(H):
        bi = hh % 2
        lr = dw(lambda e, hh=hh, bi=bi: e.dma_start(out=l16[bi][:, :], in_=d_l[hh, :, :]),
                waits=[l16_free[bi], ("ve", state["o_ve"]), ("x", ldzero_x)])
        for ch in range(4):
            bsel = ch % 2
            sidx = (hh * 4 + ch) % 4
            t_add(lambda e, bi=bi, ch=ch, bsel=bsel: e.matmul(
                bk(bsel, 0, 1, 512), ones_f[0:16, 0:1], l16[bi][0:16, ch * 512:(ch + 1) * 512],
                start=True, stop=True), waits=BK.ww([bsel]) + [("w", lr)])
            mpe = P.val("pe")
            sw = [("x", spill_hist[-4])] if len(spill_hist) >= 4 else []
            rv = v_add(lambda e, bsel=bsel, sidx=sidx: e.reciprocal(
                rl_st[0:1, sidx, :], bk(bsel, 0, 1, 512)),
                waits=[("pe", mpe)] + sw)
            BK.consumed([bsel], "ve", rv)
            spill_hist.append(dx(lambda e, hh=hh, ch=ch, sidx=sidx: e.dma_start(
                out=d_rl[hh:hh + 1, ch * 512:(ch + 1) * 512], in_=rl_st[0:1, sidx, :]),
                waits=[("ve", rv)]))
        rl_head_x[hh] = spill_hist[-1]
        l16_free[bi] = ("pe", P.val("pe"))
    rl_x = P.val("x")

    epx_free = ("x", 0)
    yatt_free = {0: 0, 1: 0}
    obf_free_pe = 0
    rt_prev = {0: 0, 32: 0, 64: 0}
    for tg in range(8):
        bi = tg % 2
        cols = slice(tg * SUB, (tg + 1) * SUB)
        xr = dx(lambda e, tg=tg: e.dma_start(out=ep_x[:, :, :],
                                             in_=xown3[:, :, tg * SUB:(tg + 1) * SUB]),
                waits=[epx_free, ("ve", state["o_ve"])])
        for hp in range(NFC):
            rbase = (0, 32, 64)[hp % 3]
            r1 = dx(lambda e, hp=hp, tg=tg, rbase=rbase: e.dma_start(
                out=rows2[rbase:rbase + 1, 512:512 + SUB],
                in_=d_rl[2 * hp:2 * hp + 1, tg * SUB:(tg + 1) * SUB]),
                waits=[("x", rl_head_x[2 * hp + 1]), ("pe", rt_prev[rbase])])
            r2 = dx(lambda e, hp=hp, tg=tg, rbase=rbase: e.dma_start(
                out=rows2[rbase:rbase + 1, 512 + SUB:512 + 2 * SUB],
                in_=d_rl[2 * hp + 1:2 * hp + 2, tg * SUB:(tg + 1) * SUB]))
            bsel = 2 + hp % 2
            def rtile(e, rbase=rbase, bsel=bsel):
                e.matmul(bk(bsel, w=SUB), sel_f[rbase:rbase + 1, 0:128],
                         rows2[rbase:rbase + 1, 512:512 + SUB], start=True, stop=False)
                return e.matmul(bk(bsel, w=SUB), sel_f[rbase:rbase + 1, 128:256],
                                rows2[rbase:rbase + 1, 512 + SUB:512 + 2 * SUB],
                                start=False, stop=True)
            t_add(rtile, waits=BK.ww([bsel]) + [("x", r2)])
            rpe = P.val("pe")
            rt_prev[rbase] = rpe
            v_add(lambda e, hp=hp, bi=bi, cols=cols, bsel=bsel: e.tensor_tensor(
                O_bf[:, bi, hp, :], O_sb[:, hp, cols], bk(bsel, w=SUB), ALU.mult),
                  waits=[("pe", max(rpe, obf_free_pe))])
            BK.consumed([bsel], "ve", P.val("ve"))
        obf_ve = P.val("ve")
        for of in range(NFC):
            bsel = 4 + of % 2
            def oproj(e, of=of, bi=bi, bsel=bsel):
                last = None
                for af in range(NFC):
                    last = e.matmul(bk(bsel, w=SUB), wkv[:, 0, af, of * 128:(of + 1) * 128],
                                    O_bf[:, bi, af, :], start=(af == 0), stop=(af == NFC - 1))
                return last
            t_add(oproj, waits=BK.ww([bsel]) + [("ve", obf_ve), ("w", wu_rdy)])
            ppe = P.val("pe")
            s_add(lambda e, of=of, bi=bi, bsel=bsel: e.activation(
                y_att[:, bi, of, :], bk(bsel, w=SUB), AF.Identity, bias=bu_col[:, of:of + 1]),
                  waits=[("pe", ppe), ("ve", bu_ve), ("x", yatt_free[bi])])
            BK.consumed([bsel], "act", P.val("act"))
        ya_act = P.val("act")
        obf_free_pe = P.val("pe")
        sumb, sqb, mb_, rb_ = (0, 1, 2, 3) if bi == 0 else (4, 5, 6, 7)
        cw = []
        wws, wwq = BK.ww([sumb]), BK.ww([sqb])
        for fc in range(NFC):
            sl = fc % 2
            v_add(lambda e, fc=fc, bi=bi, sl=sl: e.tensor_copy(ep_bc[:, sl, :], y_att[:, bi, fc, :]),
                  waits=[("act", ya_act)] + (wws if fc == 0 else [("pe", cw[fc - 2])] if fc >= 2 else []))
            bv = P.val("ve")
            t_add(lambda e, fc=fc, sl=sl, sumb=sumb: e.matmul(
                bk(sumb, 0, 1, SUB), ones_bf[:, 0:1], ep_bc[:, sl, :],
                start=(fc == 0), stop=(fc == NFC - 1)), waits=[("ve", bv)])
            cw.append(P.val("pe"))
        sum_pe = P.val("pe")
        cw2 = []
        for fc in range(NFC):
            sl = fc % 2
            v_add(lambda e, fc=fc, bi=bi, sl=sl: e.tensor_mul(
                ep_sq[:, sl, :], y_att[:, bi, fc, :], y_att[:, bi, fc, :]),
                  waits=(wwq if fc == 0 else [("pe", cw2[fc - 2])] if fc >= 2 else []))
            sv = P.val("ve")
            t_add(lambda e, fc=fc, sl=sl, sqb=sqb: e.matmul(
                bk(sqb, 0, 1, SUB), ones_bf[:, 0:1], ep_sq[:, sl, :],
                start=(fc == 0), stop=(fc == NFC - 1)), waits=[("ve", sv)])
            cw2.append(P.val("pe"))
        sq_pe = P.val("pe")
        mu_ve = v_add(lambda e, sumb=sumb: e.tensor_scalar_mul(
            rows[0:1, 0:SUB], bk(sumb, 0, 1, SUB), 1.0 / D),
            waits=[("pe", sum_pe), ("pe", rows_pe["v"])])
        v_add(lambda e, sqb=sqb: e.tensor_scalar_mul(rows[32:33, 0:SUB],
                                                     bk(sqb, 0, 1, SUB), 1.0 / D),
              waits=[("pe", sq_pe)])
        v_add(lambda e: e.tensor_mul(rows[32:33, SUB:2 * SUB], rows[0:1, 0:SUB], rows[0:1, 0:SUB]))
        var_ve = v_add(lambda e: e.tensor_sub(rows[64:65, 0:SUB], rows[32:33, 0:SUB],
                                              rows[32:33, SUB:2 * SUB]))
        s_add(lambda e: e.activation(rows[64:65, SUB:2 * SUB], rows[64:65, 0:SUB], AF.Ln, bias=EPS),
              waits=[("ve", var_ve)])
        r_act = s_add(lambda e: e.activation(rows[64:65, 0:SUB], rows[64:65, SUB:2 * SUB],
                                             AF.Exp, scale=-0.5))
        t_add(lambda e, mb_=mb_: e.matmul(bk(mb_, w=SUB), ones_f[0:1, 0:128], rows[0:1, 0:SUB],
                                          start=True, stop=True),
              waits=BK.ww([mb_]) + [("ve", mu_ve)])
        mpe = P.val("pe")
        t_add(lambda e, rb_=rb_: e.matmul(bk(rb_, w=SUB), ones_f[64:65, 0:128],
                                          rows[64:65, 0:SUB], start=True, stop=True),
              waits=BK.ww([rb_]) + [("act", r_act)])
        rpe = P.val("pe")
        rows_pe["v"] = rpe
        for fc in range(NFC):
            v_add(lambda e, fc=fc, bi=bi, mb_=mb_: e.tensor_sub(
                ep_t1[:, :], y_att[:, bi, fc, :], bk(mb_, w=SUB)),
                  waits=[("pe", max(mpe, rpe)), ("x", xr)])
            v_add(lambda e, rb_=rb_: e.tensor_tensor(ep_t2[:, :], ep_t1[:, :],
                                                     bk(rb_, w=SUB), ALU.mult))
            v_add(lambda e, fc=fc, bi=bi: e.tensor_add(y_att[:, bi, fc, :], ep_t2[:, :],
                                                       ep_x[:, fc, :]))
        x1v = P.val("ve")
        BK.consumed([sumb, sqb, mb_, rb_], "ve", x1v)
        epx_free = ("ve", x1v)
        dx(lambda e, tg=tg, bi=bi: e.dma_start(
            out=x13[:, :, tg * SUB:(tg + 1) * SUB], in_=y_att[:, bi, :, :]),
           waits=[("ve", x1v)])
        if phase == 1:
            dx(lambda e, tg=tg, bi=bi: e.dma_start(
                out=out3[:, :, tg * SUB:(tg + 1) * SUB], in_=y_att[:, bi, :, :]))
        yatt_free[bi] = P.val("x")

    if phase == 1:
        fin = P.val("x")
        P.add("sync", lambda e: e.nop(), waits=[("x", fin)])
        return _emit(nc, P)

    barrier = [("pe", P.val("pe")), ("ve", P.val("ve")), ("act", P.val("act")),
               ("x", P.val("x")), ("w", P.val("w"))]
    gw_rdy = dx(lambda e: e.dma_start(out=gwT32[:, :, :],
                                      in_=d_gwT.rearrange("(c p) m -> p c m", p=128)),
                waits=barrier)
    def gsum_mms(e):
        last = None
        for fc in range(NFC):
            last = e.matmul(bk(0, 0, 1, 8), ones_f[:, 0:1], gwT32[:, fc, :],
                            start=(fc == 0), stop=(fc == NFC - 1))
        return last
    t_add(gsum_mms, waits=BK.ww([0]) + [("x", gw_rdy)])
    gs_pe = P.val("pe")
    gs_ve = v_add(lambda e: e.tensor_copy(rows2[0:1, 0:8], bk(0, 0, 1, 8)),
                  waits=[("pe", gs_pe)])
    BK.consumed([0], "ve", gs_ve)
    t_add(lambda e: e.matmul(bk(1, w=8), ones_f[0:1, 0:128], rows2[0:1, 0:8],
                             start=True, stop=True), waits=BK.ww([1]) + [("ve", gs_ve)])
    gsb_pe = P.val("pe")
    gsum_ve = v_add(lambda e: e.tensor_copy(gsum_b[:, :], bk(1, w=8)), waits=[("pe", gsb_pe)])
    BK.consumed([1], "ve", gsum_ve)

    out_parts = []
    h_ve = 0
    hcons_pe = barrier[0][1]
    guv_prev = {}
    prev_comb_x = 0
    for half in range(2):
        hoff = half * 1024
        xf_ve = 0
        comb_ve = 0
        prev_pact = 0
        x1st_free = [("ve", 0)]
        for sub4 in range(4):
            bi = sub4 % 2
            scol = slice(sub4 * SUB, (sub4 + 1) * SUB)
            xr = dx(lambda e, sub4=sub4, hoff=hoff: e.dma_start(
                out=x1st[:, :, :], in_=x13[:, :, hoff + sub4 * SUB:hoff + (sub4 + 1) * SUB]),
                waits=barrier + x1st_free)
            sumb, sqb, mb_, rb_ = (0, 1, 2, 3) if bi == 0 else (4, 5, 6, 7)
            wws, wwq = BK.ww([sumb]), BK.ww([sqb])
            cw = []
            for fc in range(NFC):
                sl = fc % 2
                v_add(lambda e, fc=fc, sl=sl: e.tensor_copy(ybc[:, sl, :], x1st[:, fc, :]),
                      waits=[("x", xr)] + (wws if fc == 0 else [("pe", cw[fc - 2])] if fc >= 2 else []))
                bv = P.val("ve")
                t_add(lambda e, fc=fc, sl=sl, sumb=sumb: e.matmul(
                    bk(sumb, 0, 1, SUB), ones_bf[:, 0:1], ybc[:, sl, :],
                    start=(fc == 0), stop=(fc == NFC - 1)), waits=[("ve", bv)])
                cw.append(P.val("pe"))
            sum_pe = P.val("pe")
            cw2 = []
            for fc in range(NFC):
                sl = fc % 2
                v_add(lambda e, fc=fc, sl=sl: e.tensor_mul(
                    ysq[:, sl, :], x1st[:, fc, :], x1st[:, fc, :]),
                      waits=(wwq if fc == 0 else [("pe", cw2[fc - 2])] if fc >= 2 else []))
                sv = P.val("ve")
                t_add(lambda e, fc=fc, sl=sl, sqb=sqb: e.matmul(
                    bk(sqb, 0, 1, SUB), ones_bf[:, 0:1], ysq[:, sl, :],
                    start=(fc == 0), stop=(fc == NFC - 1)), waits=[("ve", sv)])
                cw2.append(P.val("pe"))
            sq_pe = P.val("pe")
            mu_ve = v_add(lambda e, sumb=sumb: e.tensor_scalar_mul(
                rows[0:1, 0:SUB], bk(sumb, 0, 1, SUB), 1.0 / D),
                waits=[("pe", sum_pe), ("pe", rows_pe["v"])])
            v_add(lambda e, sqb=sqb: e.tensor_scalar_mul(rows[32:33, 0:SUB],
                                                         bk(sqb, 0, 1, SUB), 1.0 / D),
                  waits=[("pe", sq_pe)])
            v_add(lambda e: e.tensor_mul(rows[32:33, SUB:2 * SUB],
                                         rows[0:1, 0:SUB], rows[0:1, 0:SUB]))
            var_ve = v_add(lambda e: e.tensor_sub(rows[64:65, 0:SUB], rows[32:33, 0:SUB],
                                                  rows[32:33, SUB:2 * SUB]))
            s_add(lambda e: e.activation(rows[64:65, SUB:2 * SUB], rows[64:65, 0:SUB],
                                         AF.Ln, bias=EPS), waits=[("ve", var_ve)])
            r_act = s_add(lambda e: e.activation(rows[64:65, 0:SUB], rows[64:65, SUB:2 * SUB],
                                                 AF.Exp, scale=-0.5))
            t_add(lambda e, mb_=mb_: e.matmul(bk(mb_, w=SUB), ones_f[0:1, 0:128],
                                              rows[0:1, 0:SUB], start=True, stop=True),
                  waits=BK.ww([mb_]) + [("ve", mu_ve)])
            mpe = P.val("pe")
            t_add(lambda e, rb_=rb_: e.matmul(bk(rb_, w=SUB), ones_f[64:65, 0:128],
                                              rows[64:65, 0:SUB], start=True, stop=True),
                  waits=BK.ww([rb_]) + [("act", r_act)])
            rpe = P.val("pe")
            rows_pe["v"] = rpe
            for fc in range(NFC):
                v_add(lambda e, fc=fc, mb_=mb_: e.tensor_sub(
                    tmp_e[:, 0:SUB], x1st[:, fc, :], bk(mb_, w=SUB)),
                      waits=[("pe", max(mpe, rpe))])
                v_add(lambda e, fc=fc, scol=scol, rb_=rb_: e.tensor_tensor(
                    xf[:, fc, scol], tmp_e[:, 0:SUB], bk(rb_, w=SUB), ALU.mult))
            xf_ve = P.val("ve")
            BK.consumed([sumb, sqb, mb_, rb_], "ve", xf_ve)
            if phase == 8 and half == 1 and sub4 == 3:
                dbg_pe, dbg_ve, dbg_act = P.val("pe"), P.val("ve"), P.val("act")
                dx(lambda e: e.dma_start(out=out3[:, :, 0:SUB], in_=x1st[:, :, :]),
                   waits=[("pe", dbg_pe), ("ve", dbg_ve), ("act", dbg_act)])
                dx(lambda e: e.dma_start(out=out3[:, 7, 1024:2048], in_=rows[:, :]))
                fin8 = P.val("x")
                P.add("sync", lambda e: e.nop(), waits=[("x", fin8)])
                return _emit(nc, P)
            for tc2 in ([] if phase == 6 else range(2)):
                tc = sub4 * 2 + tc2
                mw_x = BK.ww([mb_]) + [("ve", mu_ve), ("pe", P.val("pe")),
                                       ("ve", P.val("ve")), ("act", P.val("act"))]
                t_add(lambda e, tc2=tc2: e.matmul(bk(mb_, 0, 128, 1),
                                                  rows[0:1, tc2 * 128:(tc2 + 1) * 128],
                                                  ones_f[0:1, 0:1], start=True, stop=True),
                      waits=mw_x)
                mc_pe = P.val("pe")
                v_add(lambda e: e.tensor_copy(smalls[:, 8:9], bk(mb_, 0, 128, 1)),
                      waits=[("pe", mc_pe)])
                BK.consumed([mb_], "ve", P.val("ve"))
                rw_x = BK.ww([rb_]) + [("act", r_act), ("pe", P.val("pe")),
                                       ("ve", P.val("ve")), ("act", P.val("act"))]
                t_add(lambda e, tc2=tc2: e.matmul(bk(rb_, 0, 128, 1),
                                                  rows[64:65, tc2 * 128:(tc2 + 1) * 128],
                                                  ones_f[64:65, 0:1], start=True, stop=True),
                      waits=rw_x)
                rc_pe = P.val("pe")
                rows_pe["v"] = rc_pe
                v_add(lambda e: e.tensor_copy(smalls[:, 9:10], bk(rb_, 0, 128, 1)),
                      waits=[("pe", rc_pe)])
                BK.consumed([rb_], "ve", P.val("ve"))
                v_seq(lambda e: e.tensor_scalar_mul(smalls[:, 10:11], smalls[:, 9:10], -1.0))
                bsel = 4 + tc % 2 if bi == 0 else tc % 2
                def logit_mms(e, tc2=tc2, bsel=bsel):
                    last = None
                    for fc in range(NFC):
                        last = e.matmul(bk(bsel, w=8), x1st[:, fc, tc2 * 128:(tc2 + 1) * 128],
                                        gwT32[:, fc, :], start=(fc == 0), stop=(fc == NFC - 1))
                    return last
                lw_x = BK.ww([bsel]) + [("x", xr), ("ve", gsum_ve)]
                if "3" in PARANOID[0]:
                    lw_x += [("pe", P.val("pe")), ("ve", P.val("ve")), ("act", P.val("act"))]
                t_add(logit_mms, waits=lw_x)
                lg_pe = P.val("pe")
                q_ve = v_seq(lambda e, bsel=bsel: e.scalar_tensor_tensor(
                    lgt_sb[:, :], gsum_b[:, :], smalls[:, 8:9], bk(bsel, w=8),
                    ALU.mult, ALU.subtract), waits=[("pe", lg_pe), ("act", prev_pact)])
                BK.consumed([bsel], "ve", q_ve)
                v_seq(lambda e: e.tensor_scalar_mul(lgt_sb[:, :], lgt_sb[:, :],
                                                    smalls[:, 10:11]))
                v_seq(lambda e: e.reduce_max(smalls[:, 0:1], lgt_sb[:, :], AX.X, negate=True))
                nm_ve = P.val("ve")
                p_act = s_add(lambda e: e.activation(probs_sb[:, :], lgt_sb[:, :],
                                                     AF.Exp, bias=smalls[:, 0:1]),
                              waits=[("ve", nm_ve)])
                prev_pact = p_act
                v_seq(lambda e: e.reduce_sum(smalls[:, 1:2], probs_sb[:, :], AX.X),
                      waits=[("act", p_act)])
                v_seq(lambda e: e.reciprocal(smalls[:, 2:3], smalls[:, 1:2]))
                v_seq(lambda e: e.tensor_scalar_mul(probs_sb[:, :], probs_sb[:, :],
                                                    smalls[:, 2:3]))
                v_seq(lambda e: e.reduce_max(smalls[:, 3:4], probs_sb[:, :], AX.X))
                v_seq(lambda e: e.tensor_single_scalar(eq_sb[:, :], probs_sb[:, :],
                                                       smalls[:, 3:4], ALU.is_ge))
                v_seq(lambda e: e.scalar_tensor_tensor(eq_sb[:, :], eq_sb[:, :], -2.0,
                                                       probs_sb[:, :], ALU.mult, ALU.add))
                v_seq(lambda e: e.reduce_max(smalls[:, 4:5], eq_sb[:, :], AX.X))
                v_seq(lambda e: e.tensor_single_scalar(eq_sb[:, :], probs_sb[:, :],
                                                       smalls[:, 4:5], ALU.is_ge))
                v_seq(lambda e: e.tensor_add(smalls[:, 5:6], smalls[:, 3:4], smalls[:, 4:5]))
                v_seq(lambda e: e.tensor_scalar_add(smalls[:, 5:6], smalls[:, 5:6], 1e-20))
                v_seq(lambda e: e.reciprocal(smalls[:, 6:7], smalls[:, 5:6]))
                cs_ve = v_seq(lambda e: e.scalar_tensor_tensor(
                    comb_sb[:, :], probs_sb[:, :], smalls[:, 6:7], eq_sb[:, :],
                    ALU.mult, ALU.mult))
                bsel2 = (6 if bi == 0 else 2) + tc % 2
                tw_x = BK.ww([bsel2]) + [("ve", cs_ve), ("x", ident_rdy)]
                if "5" in PARANOID[0]:
                    tw_x += [("pe", P.val("pe")), ("ve", P.val("ve")), ("act", P.val("act"))]
                t_add(lambda e, bsel2=bsel2: e.transpose(bk(bsel2, 0, 8, 128), comb_sb[:, :],
                                                         ident_sb[:, :]),
                      waits=tw_x)
                tp_pe = P.val("pe")
                comb_ve = v_add(lambda e, tc=tc, bsel2=bsel2: e.tensor_copy(
                    combT[0:8, tc * 128:(tc + 1) * 128], bk(bsel2, 0, 8, 128)),
                    waits=[("pe", tp_pe), ("x", prev_comb_x)])
                BK.consumed([bsel2], "ve", comb_ve)
            x1st_free = [("pe", P.val("pe")), ("ve", P.val("ve"))]
        for e8 in range(NE):
            rb8, cb8 = (e8 // 3) * 32, (e8 % 3) * 1024
            dx(lambda e, e8=e8, rb8=rb8, cb8=cb8: e.dma_start(
                out=comb_rows[rb8:rb8 + 1, cb8:cb8 + 1024], in_=combT[e8:e8 + 1, :]),
               waits=[("ve", comb_ve), ("pe", hcons_pe)])
        comb_x = P.val("x")
        prev_comb_x = comb_x

        dly = [("x", P.val("x")), ("ve", P.val("ve")), ("act", P.val("act"))] if phase == 7 else []
        for ex in [NE] + list(range(NE)):
            shared = (ex == NE)
            dsrc = d_sdT if shared else d_edT[ex]
            gsrc = d_sgT if shared else d_egT[ex]
            usrc = d_suT if shared else d_euT[ex]
            dw_rdy = None
            for mc in range(22):
                bi2 = mc % 2
                gwx = [guv_prev.get(("g", bi2), ("pe", hcons_pe))] + dly + barrier
                if "2" in PARANOID[0]:
                    gwx += [("pe", P.val("pe")), ("ve", P.val("ve")), ("act", P.val("act"))]
                gr = dw(lambda e, gsrc=gsrc, mc=mc, bi2=bi2: e.dma_start(
                    out=gwc[:, bi2, :, :],
                    in_=gsrc.rearrange("(c p) m -> p c m", p=128)[:, :, mc * 128:(mc + 1) * 128]),
                    waits=gwx)
                uwx = [guv_prev.get(("u", bi2), ("pe", hcons_pe))] + dly + barrier
                if "2" in PARANOID[0]:
                    uwx += [("pe", P.val("pe")), ("ve", P.val("ve")), ("act", P.val("act"))]
                ur = dw(lambda e, usrc=usrc, mc=mc, bi2=bi2: e.dma_start(
                    out=uwc[:, bi2, :, :],
                    in_=usrc.rearrange("(c p) m -> p c m", p=128)[:, :, mc * 128:(mc + 1) * 128]),
                    waits=uwx)
                if mc < 11:
                    dw_rdy = dw(lambda e, dsrc=dsrc, mc=mc: e.dma_start(
                        out=dw_sb[:, 2 * mc:2 * mc + 2, :],
                        in_=dsrc.rearrange("(c p) m -> p c m", p=128)[:, 2 * mc:2 * mc + 2, :]),
                        waits=([("pe", hcons_pe)] + dly + barrier) if mc == 0 else [])
                gb = (0, 1) if bi2 == 0 else (4, 5)
                ub = (2, 3) if bi2 == 0 else (6, 7)
                def g_mms(e, bi2=bi2, gb=gb):
                    last = None
                    for tq in range(2):
                        for fc in range(NFC):
                            last = e.matmul(bk(gb[tq]), gwc[:, bi2, fc, :],
                                            xf[:, fc, tq * 512:(tq + 1) * 512],
                                            start=(fc == 0), stop=(fc == NFC - 1))
                    return last
                t_add(g_mms, waits=BK.ww(list(gb)) + [("w", gr), ("ve", xf_ve)])
                g_pe = P.val("pe")
                guv_prev[("g", bi2)] = ("pe", g_pe)
                def u_mms(e, bi2=bi2, ub=ub):
                    last = None
                    for tq in range(2):
                        for fc in range(NFC):
                            last = e.matmul(bk(ub[tq]), uwc[:, bi2, fc, :],
                                            xf[:, fc, tq * 512:(tq + 1) * 512],
                                            start=(fc == 0), stop=(fc == NFC - 1))
                    return last
                t_add(u_mms, waits=BK.ww(list(ub)) + [("w", ur)])
                u_pe = P.val("pe")
                guv_prev[("u", bi2)] = ("pe", u_pe)
                sg_act = s_add(lambda e, gb=gb: e.activation(
                    sg_sb[:, :], pa[:, gb[0] * 512:gb[0] * 512 + 1024], AF.Silu),
                    waits=[("pe", g_pe), ("ve", h_ve)])
                BK.consumed(list(gb), "act", sg_act)
                h_ve = v_add(lambda e, mc=mc, ub=ub: e.tensor_tensor(
                    h_sb[:, mc, :], sg_sb[:, :], pa[:, ub[0] * 512:ub[0] * 512 + 1024], ALU.mult),
                    waits=[("act", sg_act), ("pe", max(u_pe, hcons_pe))])
                BK.consumed(list(ub), "ve", h_ve)
            ct_ve = 0
            if not shared:
                rbase = (0, 32, 64)[ex // 3]
                ccol = (ex % 3) * 1024
                for tq in range(2):
                    bsel = 2 + tq
                    t_add(lambda e, rbase=rbase, ccol=ccol, tq=tq, bsel=bsel: e.matmul(
                        bk(bsel), ones_f[rbase:rbase + 1, 0:128],
                        comb_rows[rbase:rbase + 1, ccol + tq * 512:ccol + (tq + 1) * 512],
                        start=True, stop=True), waits=BK.ww([bsel]) + [("x", comb_x)])
                    cpe = P.val("pe")
                    ct_ve = v_add(lambda e, tq=tq, bsel=bsel: e.tensor_copy(
                        ctile[:, tq, :], bk(bsel)), waits=[("pe", cpe)])
                    BK.consumed([bsel], "ve", ct_ve)
            for mc8 in range(NFC):
                for tq in range(2):
                    bsel = (mc8 * 2 + tq) % 8
                    def d_mms(e, mc8=mc8, tq=tq, bsel=bsel):
                        last = None
                        for kc in range(22):
                            last = e.matmul(bk(bsel), dw_sb[:, kc, mc8 * 128:(mc8 + 1) * 128],
                                            h_sb[:, kc, tq * 512:(tq + 1) * 512],
                                            start=(kc == 0), stop=(kc % 8 == 7 or kc == 21),
                                            skip_group_check=True)
                        return last
                    t_add(d_mms, waits=BK.ww([bsel]) + [("ve", h_ve), ("w", dw_rdy)])
                    dpe = P.val("pe")
                    if shared:
                        v_add(lambda e, mc8=mc8, tq=tq, bsel=bsel: e.tensor_copy(
                            y_acc[:, mc8, tq * 512:(tq + 1) * 512], bk(bsel)),
                              waits=[("pe", dpe)])
                    else:
                        v_add(lambda e, tq=tq, bsel=bsel: e.tensor_tensor(
                            tmp_e[:, :], bk(bsel), ctile[:, tq, :], ALU.mult),
                              waits=[("pe", dpe), ("ve", ct_ve)])
                        v_add(lambda e, mc8=mc8, tq=tq: e.tensor_add(
                            y_acc[:, mc8, tq * 512:(tq + 1) * 512],
                            y_acc[:, mc8, tq * 512:(tq + 1) * 512], tmp_e[:, :]))
                    BK.consumed([bsel], "ve", P.val("ve"))
            hcons_pe = P.val("pe")
        y_done_ve = P.val("ve")

        if phase in (4, 5, 6, 7) and half == 1:
            dbg_pe, dbg_ve = P.val("pe"), P.val("ve")
            if phase == 6:
                dx(lambda e: e.dma_start(out=out3[:, :, :], in_=x13[:, :, :]),
                   waits=[("pe", dbg_pe), ("ve", dbg_ve)])
            if phase in (4, 7):
                yv = y_acc.rearrange("p a b -> p (a b)").rearrange("p (c d) -> p c d", d=2048)
                xv = xf.bitcast(F32).rearrange("p a b -> p (a b)").rearrange("p (c d) -> p c d", d=2048)
                dx(lambda e: e.dma_start(out=out3[:, 0:4, :], in_=yv[:, :, :]),
                   waits=[("pe", dbg_pe), ("ve", dbg_ve)])
                dx(lambda e: e.dma_start(out=out3[:, 4:6, :], in_=xv[:, :, :]),
                   waits=[("pe", dbg_pe), ("ve", dbg_ve)])
            if phase == 5:
                dx(lambda e: e.dma_start(out=out3[:, 0, 0:1536], in_=comb_rows[:, 0:1536]),
                   waits=[("pe", dbg_pe), ("ve", dbg_ve)])
                dx(lambda e: e.dma_start(out=out3[:, 1, 0:1536], in_=comb_rows[:, 1536:3072]),
                   waits=[("pe", dbg_pe), ("ve", dbg_ve)])
                hv8 = h_sb.bitcast(F32).rearrange("p a b -> p (a b)")
                dx(lambda e: e.dma_start(
                    out=out3[:, 2:6, :].rearrange("p c t -> p (c t)")[:, 0:8192],
                    in_=hv8[:, 0:8192]),
                   waits=[("pe", dbg_pe), ("ve", dbg_ve)]) if False else None
            fin = P.val("x")
            P.add("sync", lambda e: e.nop(), waits=[("x", fin)])
            return _emit(nc, P)
        if phase in (4, 5, 6, 7):
            continue
        x1st_free = [("ve", P.val("ve"))]
        for sub4 in range(4):
            bi = sub4 % 2
            scol = slice(sub4 * SUB, (sub4 + 1) * SUB)
            xr = dx(lambda e, sub4=sub4, hoff=hoff: e.dma_start(
                out=x1st[:, :, :], in_=x13[:, :, hoff + sub4 * SUB:hoff + (sub4 + 1) * SUB]),
                waits=x1st_free)
            sumb, sqb, mb_, rb_ = (0, 1, 2, 3) if bi == 0 else (4, 5, 6, 7)
            wws, wwq = BK.ww([sumb]), BK.ww([sqb])
            cw = []
            for fc in range(NFC):
                sl = fc % 2
                v_add(lambda e, fc=fc, sl=sl, scol=scol: e.tensor_copy(
                    ybc[:, sl, :], y_acc[:, fc, scol]),
                      waits=[("ve", y_done_ve)] + (wws if fc == 0 else
                                                   [("pe", cw[fc - 2])] if fc >= 2 else []))
                bv = P.val("ve")
                t_add(lambda e, fc=fc, sl=sl, sumb=sumb: e.matmul(
                    bk(sumb, 0, 1, SUB), ones_bf[:, 0:1], ybc[:, sl, :],
                    start=(fc == 0), stop=(fc == NFC - 1)), waits=[("ve", bv)])
                cw.append(P.val("pe"))
            sum_pe = P.val("pe")
            cw2 = []
            for fc in range(NFC):
                sl = fc % 2
                v_add(lambda e, fc=fc, sl=sl, scol=scol: e.tensor_mul(
                    ysq[:, sl, :], y_acc[:, fc, scol], y_acc[:, fc, scol]),
                      waits=(wwq if fc == 0 else [("pe", cw2[fc - 2])] if fc >= 2 else []))
                sv = P.val("ve")
                t_add(lambda e, fc=fc, sl=sl, sqb=sqb: e.matmul(
                    bk(sqb, 0, 1, SUB), ones_bf[:, 0:1], ysq[:, sl, :],
                    start=(fc == 0), stop=(fc == NFC - 1)), waits=[("ve", sv)])
                cw2.append(P.val("pe"))
            sq_pe = P.val("pe")
            mu_ve = v_add(lambda e, sumb=sumb: e.tensor_scalar_mul(
                rows[0:1, 0:SUB], bk(sumb, 0, 1, SUB), 1.0 / D),
                waits=[("pe", sum_pe), ("pe", rows_pe["v"])])
            v_add(lambda e, sqb=sqb: e.tensor_scalar_mul(rows[32:33, 0:SUB],
                                                         bk(sqb, 0, 1, SUB), 1.0 / D),
                  waits=[("pe", sq_pe)])
            v_add(lambda e: e.tensor_mul(rows[32:33, SUB:2 * SUB],
                                         rows[0:1, 0:SUB], rows[0:1, 0:SUB]))
            var_ve = v_add(lambda e: e.tensor_sub(rows[64:65, 0:SUB], rows[32:33, 0:SUB],
                                                  rows[32:33, SUB:2 * SUB]))
            s_add(lambda e: e.activation(rows[64:65, SUB:2 * SUB], rows[64:65, 0:SUB],
                                         AF.Ln, bias=EPS), waits=[("ve", var_ve)])
            r_act = s_add(lambda e: e.activation(rows[64:65, 0:SUB], rows[64:65, SUB:2 * SUB],
                                                 AF.Exp, scale=-0.5))
            t_add(lambda e, mb_=mb_: e.matmul(bk(mb_, w=SUB), ones_f[0:1, 0:128],
                                              rows[0:1, 0:SUB], start=True, stop=True),
                  waits=BK.ww([mb_]) + [("ve", mu_ve)])
            mpe = P.val("pe")
            t_add(lambda e, rb_=rb_: e.matmul(bk(rb_, w=SUB), ones_f[64:65, 0:128],
                                              rows[64:65, 0:SUB], start=True, stop=True),
                  waits=BK.ww([rb_]) + [("act", r_act)])
            rpe = P.val("pe")
            rows_pe["v"] = rpe
            for fc in range(NFC):
                v_add(lambda e, fc=fc, scol=scol, mb_=mb_: e.tensor_sub(
                    tmp_e[:, 0:SUB], y_acc[:, fc, scol], bk(mb_, w=SUB)),
                      waits=[("pe", max(mpe, rpe)), ("x", xr)])
                v_add(lambda e, rb_=rb_: e.tensor_tensor(
                    tmp_e[:, 0:SUB], tmp_e[:, 0:SUB], bk(rb_, w=SUB), ALU.mult))
                v_add(lambda e, fc=fc: e.tensor_add(x1st[:, fc, :], tmp_e[:, 0:SUB],
                                                    x1st[:, fc, :]))
            x2v = P.val("ve")
            BK.consumed([sumb, sqb, mb_, rb_], "ve", x2v)
            ov = dx(lambda e, sub4=sub4, hoff=hoff: e.dma_start(
                out=out3[:, :, hoff + sub4 * SUB:hoff + (sub4 + 1) * SUB], in_=x1st[:, :, :]),
                waits=[("ve", x2v)])
            x1st_free = [("x", ov)]
            out_parts.append(ov)

    fin = P.val("x")
    P.add("sync", lambda e: e.nop(), waits=[("x", fin)])
    return _emit(nc, P)


def _emit(nc, P):
    with nc.semaphore("s_x") as s_x, nc.semaphore("s_w") as s_w, \
         nc.semaphore("s_pe") as s_pe, nc.semaphore("s_act") as s_act, \
         nc.semaphore("s_ve") as s_ve, nc.Block() as blk:
        sem_map = {"x": s_x, "w": s_w, "pe": s_pe, "act": s_act, "ve": s_ve}

        def emit(eng_name):
            def body(eng):
                for wl, fn, inc in P.steps[eng_name]:
                    for s, v in wl:
                        eng.wait_ge(sem_map[s], v)
                    inst = fn(eng)
                    if inc is not None:
                        inst.then_inc(sem_map[inc[0]], inc[1])
            return body

        blk.sync(emit("sync"))
        blk.gpsimd(emit("gpsimd"))
        blk.tensor(emit("tensor"))
        blk.vector(emit("vector"))
        blk.scalar(emit("scalar"))
    return nc


def host_inputs(inputs):
    g = lambda k: np.ascontiguousarray(np.asarray(inputs[k], dtype=np.float32))
    x = g("L0_attn_x")
    xfm = np.ascontiguousarray(x.transpose(0, 2, 1))
    wq, wk, wv, wu = g("L0_attn_w_q"), g("L0_attn_w_k"), g("L0_attn_w_v"), g("L0_attn_w_u")
    bu = g("L0_attn_b_u").reshape(1, D)
    gwT = np.ascontiguousarray(g("L0_ffn_gate_w").T)
    egT = np.ascontiguousarray(g("L0_ffn_expert_gate").transpose(0, 2, 1))
    euT = np.ascontiguousarray(g("L0_ffn_expert_up").transpose(0, 2, 1))
    edT = np.ascontiguousarray(g("L0_ffn_expert_down").transpose(0, 2, 1))
    sgT = np.ascontiguousarray(g("L0_ffn_shared_gate").T)
    suT = np.ascontiguousarray(g("L0_ffn_shared_up").T)
    sdT = np.ascontiguousarray(g("L0_ffn_shared_down").T)
    ident = np.eye(128, dtype=np.float32)

    in_maps, metas = [], []
    for c in range(NCORES):
        seq, r = c // 4, c % 4
        tiles = [4 * (e - 1) + r for e in range(16, 0, -1)]
        cols = np.concatenate([np.arange(i * 128, (i + 1) * 128) for i in tiles])
        metas.append((seq, cols))
        xown = np.ascontiguousarray(xfm[seq][:, cols])
        p = np.arange(128)[None, :, None]
        cq = np.arange(128)[None, None, :]
        kcc = np.arange(4)[:, None, None]
        m = (r * 128 + cq >= kcc * 128 + p).astype(np.float32)
        mask = np.ascontiguousarray(m.transpose(1, 0, 2).reshape(128, 512)).astype(
            ml_dtypes.bfloat16)
        in_maps.append({
            "xseq": xfm[seq], "xown": xown, "mask": mask, "ident": ident,
            "wq": wq, "wk": wk, "wv": wv, "wu": wu, "bu": bu, "gwT": gwT,
            "egT": egT, "euT": euT, "edT": edT, "sgT": sgT, "suT": suT, "sdT": sdT,
        })
    return in_maps, metas


def run_device(inputs, phase=2, nb=NB):
    in_maps, metas = host_inputs(inputs)
    nc = build_nc(phase=phase, nb=nb)
    res = run_bass_kernel_spmd(nc, in_maps, core_ids=list(range(NCORES)))
    out = np.empty((B, S, D), dtype=np.float32)
    for c in range(NCORES):
        seq, cols = metas[c]
        out[seq, cols, :] = res.results[c]["xout"].T
    return out


def kernel(**inputs):
    return run_device(inputs, phase=2, nb=NB)
